# version 01
# baseline (speedup 1.0000x reference)
import numpy as np
import ml_dtypes
from contextlib import ExitStack

import concourse.bass as bass
import concourse.mybir as mybir
from concourse.bass_utils import run_bass_kernel_spmd

F32 = mybir.dt.float32
BF16 = mybir.dt.bfloat16
AF = mybir.ActivationFunctionType
ALU = mybir.AluOpType
AX = mybir.AxisListType.X

D = 2048
HH = 5632
NB = 4
SEQ = 8192
T = 4096
G = 4
GT = 1024
NSUB = 8
EC = 16
HC = 44
FQ = 4
DENOM = float(np.float32(np.sqrt(np.float32(2048.0))) / np.float32(8192.0))

ENGS = ("sync", "tensor", "vector", "scalar", "gpsimd")


class Prog:
    def __init__(self):
        self.ops = {e: [] for e in ENGS}
        self.c = {}
        self.sems = {}

    def sem(self, name):
        self.c.setdefault(name, 0)
        return name

    def raw(self, eng, fn):
        self.ops[eng].append(fn)

    def wait(self, eng, sem, val):
        if not val or val <= 0:
            return
        def f(q, sem=sem, val=val):
            q.wait_ge(self.sems[sem], val)
        self.raw(eng, f)

    def dma(self, eng, sem, out, in_):
        self.sem(sem)
        self.c[sem] += 16
        def f(q, out=out, in_=in_, sem=sem):
            q.dma_start(out=out, in_=in_).then_inc(self.sems[sem], 16)
        self.raw(eng, f)
        return self.c[sem]

    def op(self, eng, sem, fn):
        self.sem(sem)
        prev = self.c[sem]
        if prev:
            self.wait(eng, sem, prev)
        self.c[sem] += 1
        def f(q, fn=fn, sem=sem):
            fn(q).then_inc(self.sems[sem], 1)
        self.raw(eng, f)
        return self.c[sem]

    def mm(self, fn, inc=0):
        self.sem("pe")
        if inc:
            self.c["pe"] += inc
        def f(q, fn=fn, inc=inc):
            ins = fn(q)
            if inc:
                ins.then_inc(self.sems["pe"], inc)
        self.raw("tensor", f)
        return self.c["pe"]


def build_nc():
    nc = bass.Bass("TRN2", target_bir_lowering=False)

    xin = nc.dram_tensor("xin", [T, D], F32, kind="ExternalInput").ap()
    xprev = nc.dram_tensor("xprev", [T, D], F32, kind="ExternalInput").ap()
    wq = nc.dram_tensor("wq", [D, D], F32, kind="ExternalInput").ap()
    wk = nc.dram_tensor("wk", [D, D], F32, kind="ExternalInput").ap()
    wv = nc.dram_tensor("wv", [D, D], F32, kind="ExternalInput").ap()
    wo = nc.dram_tensor("wo", [D, D], F32, kind="ExternalInput").ap()
    w1 = nc.dram_tensor("w1", [HH, D], F32, kind="ExternalInput").ap()
    w2 = nc.dram_tensor("w2", [HH, D], F32, kind="ExternalInput").ap()
    w3 = nc.dram_tensor("w3", [D, HH], F32, kind="ExternalInput").ap()
    eye_f = nc.dram_tensor("eye_f", [128, 128], F32, kind="ExternalInput").ap()
    eye_bi = nc.dram_tensor("eye_bi", [128, 128], BF16, kind="ExternalInput").ap()
    ones_i = nc.dram_tensor("ones_i", [1, 128], F32, kind="ExternalInput").ap()
    out = nc.dram_tensor("out", [T, D], F32, kind="ExternalOutput").ap()

    s_wqkv = nc.dram_tensor("s_wqkv", [3, EC, 128, EC, 128], BF16, kind="Internal").ap()
    s_wo = nc.dram_tensor("s_wo", [FQ, EC, 128, 512], BF16, kind="Internal").ap()
    s_w1 = nc.dram_tensor("s_w1", [HC, 128, EC, 128], BF16, kind="Internal").ap()
    s_w2 = nc.dram_tensor("s_w2", [HC, 128, EC, 128], BF16, kind="Internal").ap()
    s_w3 = nc.dram_tensor("s_w3", [FQ, HC, 128, 512], BF16, kind="Internal").ap()
    s_red = nc.dram_tensor("s_red", [3, 128], F32, kind="Internal").ap()

    p = Prog()

    with ExitStack() as ctx:
        e = ctx.enter_context

        xnT = e(nc.sbuf_tensor("xnT", [128, EC * GT], BF16))
        hidv = e(nc.sbuf_tensor("hidv", [128, HC * GT], BF16))
        acc = e(nc.sbuf_tensor("acc", [128, NSUB * D], BF16))
        x_f = [e(nc.sbuf_tensor(f"x_f{i}", [128, D], F32)) for i in range(2)]
        xtk = e(nc.sbuf_tensor("xtk", [128, D], BF16))
        rql = [e(nc.sbuf_tensor(f"rql{i}", [128, D], BF16)) for i in range(3)]
        rwo = [e(nc.sbuf_tensor(f"rwo{i}", [128, 512], BF16)) for i in range(3)]
        outst = [e(nc.sbuf_tensor(f"outst{i}", [128, 512], F32)) for i in range(2)]
        qtmp = e(nc.sbuf_tensor("qtmp", [128, GT], BF16))
        ktmp = e(nc.sbuf_tensor("ktmp", [128, GT], BF16))
        kvb = e(nc.sbuf_tensor("kvb", [128, GT], BF16))
        kvt = e(nc.sbuf_tensor("kvt", [128, GT], BF16))
        sgt = e(nc.sbuf_tensor("sgt", [128, GT], BF16))
        eyeF = e(nc.sbuf_tensor("eyeF", [128, 128], F32))
        eyeB = e(nc.sbuf_tensor("eyeB", [128, 128], BF16))
        onesrow = e(nc.sbuf_tensor("onesrow", [1, 128], F32))
        sones = e(nc.sbuf_tensor("sones", [128, GT], BF16))
        carA = e(nc.sbuf_tensor("carA", [128, EC], F32))
        carB = e(nc.sbuf_tensor("carB", [128, EC], F32))
        tall = e(nc.sbuf_tensor("tall", [128, 4], F32))
        tneg = e(nc.sbuf_tensor("tneg", [128, 4], F32))
        epst = e(nc.sbuf_tensor("epst", [128, 1], F32))
        stats = e(nc.sbuf_tensor("stats", [128, 24], F32))
        mv = e(nc.sbuf_tensor("mv", [128, 2 * NSUB], F32))
        rstd = e(nc.sbuf_tensor("rstd", [128, NSUB], F32))
        rstd2 = e(nc.sbuf_tensor("rstd2", [128, NSUB], F32))
        pg = e(nc.sbuf_tensor("pg", [128, 136], F32))
        pgs = e(nc.sbuf_tensor("pgs", [128, 1], F32))
        redrow = e(nc.sbuf_tensor("redrow", [1, 128], F32))
        tsc = e(nc.sbuf_tensor("tsc", [1, 4], F32))
        tmp1 = e(nc.sbuf_tensor("tmp1", [128, 1], F32))

        psA = e(nc.psum_tensor("psA", [128, 2048], F32))
        psB = e(nc.psum_tensor("psB", [128, 2048], F32))
        psX = [psA, psB]
        vhat = hidv

        for s in ("ld0", "ldm", "gp0", "gpm", "pe", "dv", "ac", "pl"):
            p.sems[s] = e(nc.semaphore(s))
            p.sem(s)

        p.dma("sync", "ld0", eyeF[:], eye_f)
        p.dma("sync", "ld0", eyeB[:], eye_bi)
        p.dma("sync", "ld0", onesrow[:], ones_i)
        p.wait("vector", "ld0", 48)
        p.op("vector", "dv", lambda v: v.memset(epst[:], 1e-5))
        p.op("vector", "dv", lambda v: v.memset(sones[:], 1.0))
        p.op("vector", "dv", lambda v: v.memset(carA[:], 0.0))
        p.op("vector", "dv", lambda v: v.memset(carB[:], 0.0))

        xf_free_pe = [0, 0]
        xf_free_dv = [0, 0]
        xf_free_pl = [0, 0]
        rql_free_gp = [0, 0, 0]
        rql_free_pe = [0, 0, 0]
        rwo_free_pe = [0, 0, 0]
        ps_free_ac = [0, 0]
        ps_free_dv = [0, 0]
        outst_free_gp = [0, 0]
        xtk_free_pe = 0
        stg_i = 0

        def prep_attn_strip(src_rows, dst_contig, dst_scatter, idx):
            nonlocal stg_i
            sl = idx % 2
            par = idx % 2
            stg = stg_i % 3
            stg_i += 1
            p.wait("sync", "pe", xf_free_pe[sl])
            lv = p.dma("sync", "ld0", x_f[sl][:], src_rows)
            p.wait("tensor", "ld0", lv)
            p.wait("tensor", "ac", ps_free_ac[par])
            p.wait("tensor", "dv", ps_free_dv[par])
            for c in range(16):
                p.mm(lambda t, c=c, par=par, sl=sl: t.transpose(
                    psX[par][:, c * 128:(c + 1) * 128],
                    x_f[sl][:, c * 128:(c + 1) * 128], eyeF[:]),
                    inc=1 if c == 15 else 0)
            pv = p.c["pe"]
            xf_free_pe[sl] = pv
            p.wait("scalar", "pe", pv)
            p.wait("scalar", "gp0", rql_free_gp[stg])
            av = p.op("scalar", "ac", lambda s, stg=stg, par=par: s.copy(
                rql[stg][:], psX[par][:]))
            ps_free_ac[par] = av
            p.wait("gpsimd", "ac", av)
            if dst_contig is not None:
                gv = p.dma("gpsimd", "gp0", dst_contig, rql[stg][:])
            else:
                for (dap, c) in dst_scatter:
                    gv = p.dma("gpsimd", "gp0", dap, rql[stg][:, c * 128:(c + 1) * 128])
            rql_free_gp[stg] = gv

        for pi, W in enumerate((wq, wk, wv)):
            for ec in range(EC):
                prep_attn_strip(W[ec * 128:(ec + 1) * 128, :],
                                s_wqkv[pi, ec].rearrange("p c n -> p (c n)"), None,
                                pi * EC + ec)
        for fc in range(EC):
            scat = [(s_wo[fc // 4, ec2][:, (fc % 4) * 128:(fc % 4) * 128 + 128], ec2)
                    for ec2 in range(EC)]
            prep_attn_strip(wo[fc * 128:(fc + 1) * 128, :], None, scat, 48 + fc)

        col = 0
        gamma_jobs = [
            (w1, [(r, 0, D) for r in range(HC)]),
            (w2, [(r, 0, D) for r in range(HC)]),
            (w3, [(r, j * D, (D if j < 2 else HH - 2 * D)) for r in range(EC) for j in range(3)]),
        ]
        for mi, (W, subs) in enumerate(gamma_jobs):
            c0 = col
            for (r, c_off, width) in subs:
                sl = col % 2
                p.wait("sync", "dv", xf_free_dv[sl])
                p.wait("sync", "pe", xf_free_pe[sl])
                lv = p.dma("sync", "ld0", x_f[sl][:, 0:width],
                           W[r * 128:(r + 1) * 128, c_off:c_off + width])
                p.wait("vector", "ld0", lv)
                dvv = p.op("vector", "dv", lambda v, sl=sl, col=col, width=width:
                           v.tensor_reduce(pg[:, col:col + 1], x_f[sl][:, 0:width],
                                           AX, ALU.add, apply_absolute_value=True))
                xf_free_dv[sl] = dvv
                col += 1
            p.op("vector", "dv", lambda v, c0=c0, c1=col: v.tensor_reduce(
                pgs[:], pg[:, c0:c1], AX, ALU.add))
            p.wait("gpsimd", "dv", p.c["dv"])
            gv = p.dma("gpsimd", "gpm", s_red[mi].rearrange("(pp u) -> pp u", u=1), pgs[:])
            p.wait("sync", "gpm", gv)
            lv = p.dma("sync", "ld0", redrow[:], s_red[mi].rearrange("(u pp) -> u pp", u=1))
            p.wait("vector", "ld0", lv)
            n_el = float(HH * D)
            p.op("vector", "dv", lambda v, mi=mi: v.tensor_reduce(
                tsc[:, mi:mi + 1], redrow[:], AX, ALU.add))
            p.op("vector", "dv", lambda v, mi=mi, n_el=n_el: v.tensor_scalar(
                tsc[:, mi:mi + 1], tsc[:, mi:mi + 1], 0.5 / n_el, 0.5e-8, ALU.mult, ALU.add))
            p.wait("tensor", "dv", p.c["dv"])
            p.wait("tensor", "ac", ps_free_ac[0])
            p.wait("tensor", "dv", ps_free_dv[0])
            p.mm(lambda t, mi=mi: t.matmul(psA[:, 0:1], onesrow[:], tsc[0:1, mi:mi + 1],
                                           start=True, stop=True), inc=1)
            pv = p.c["pe"]
            p.wait("scalar", "pe", pv)
            av = p.op("scalar", "ac", lambda s, mi=mi: s.copy(tall[:, mi:mi + 1], psA[:, 0:1]))
            ps_free_ac[0] = av
            p.wait("vector", "ac", av)
            p.op("vector", "dv", lambda v, mi=mi: v.tensor_scalar(
                tneg[:, mi:mi + 1], tall[:, mi:mi + 1], -1.0, None, ALU.mult))

        def quant_strip(src_ap, width, m, dst_contig, dst_scatter, idx):
            nonlocal stg_i
            sl = idx % 2
            par = idx % 2
            stg = stg_i % 3
            stg_i += 1
            p.wait("sync", "dv", xf_free_dv[sl])
            p.wait("sync", "pe", xf_free_pe[sl])
            lv = p.dma("sync", "ld0", x_f[sl][:, 0:width], src_ap)
            nt = width // 128
            p.wait("tensor", "ld0", lv)
            p.wait("tensor", "ac", ps_free_ac[par])
            p.wait("tensor", "dv", ps_free_dv[par])
            for c in range(nt):
                p.mm(lambda t, c=c, par=par, sl=sl: t.transpose(
                    psX[par][:, c * 128:(c + 1) * 128],
                    x_f[sl][:, c * 128:(c + 1) * 128], eyeF[:]),
                    inc=1 if c == nt - 1 else 0)
            pv = p.c["pe"]
            xf_free_pe[sl] = pv
            p.wait("vector", "pe", pv)
            p.wait("vector", "gp0", rql_free_gp[stg])
            p.op("vector", "dv", lambda v, par=par, m=m, width=width: v.tensor_scalar(
                xtk[:, 0:width], psX[par][:, 0:width], tneg[:, m:m + 1], None, ALU.is_lt))
            dvv = p.op("vector", "dv", lambda v, par=par, m=m, width=width, stg=stg:
                       v.scalar_tensor_tensor(rql[stg][:, 0:width], psX[par][:, 0:width],
                                              tall[:, m:m + 1], xtk[:, 0:width],
                                              ALU.is_gt, ALU.subtract))
            ps_free_dv[par] = dvv
            xf_free_dv[sl] = dvv
            p.wait("gpsimd", "dv", dvv)
            if dst_contig is not None:
                gv = p.dma("gpsimd", "gp0", dst_contig, rql[stg][:, 0:width])
            else:
                for (dap, c) in dst_scatter:
                    gv = p.dma("gpsimd", "gp0", dap, rql[stg][:, c * 128:(c + 1) * 128])
            rql_free_gp[stg] = gv

        idx = 0
        for mi, W, dst in [(0, w1, s_w1), (1, w2, s_w2)]:
            for hc in range(HC):
                quant_strip(W[hc * 128:(hc + 1) * 128, :], D, mi,
                            dst[hc].rearrange("p c n -> p (c n)"), None, idx)
                idx += 1
        for fc in range(EC):
            for j in range(3):
                width = D if j < 2 else HH - 2 * D
                hc0 = j * 16
                scat = [(s_w3[fc // 4, hc0 + c][:, (fc % 4) * 128:(fc % 4) * 128 + 128], c)
                        for c in range(width // 128)]
                quant_strip(w3[fc * 128:(fc + 1) * 128, j * D:j * D + width], width, 2,
                            None, scat, idx)
                idx += 1

        gp0_all_done = p.c["gp0"]
        acc_free_pe = [0]

        def ln_sub(src_rows, sub, copy_acc, acc_off, ldsem):
            nonlocal xtk_free_pe
            sl = sub % 2
            par = sub % 2
            p.wait("sync", "dv", xf_free_dv[sl])
            p.wait("sync", "pl", xf_free_pl[sl])
            lv = p.dma("sync", ldsem, x_f[sl][:], src_rows)
            p.wait("vector", ldsem, lv)
            for j in range(4):
                p.op("vector", "dv", lambda v, sl=sl, j=j: v.bn_stats(
                    stats[:, j * 6:(j + 1) * 6], x_f[sl][:, j * 512:(j + 1) * 512]))
            p.op("vector", "dv", lambda v, sub=sub: v.bn_aggr(
                mv[:, 2 * sub:2 * sub + 2], stats[:]))
            p.wait("scalar", "dv", p.c["dv"])
            av = p.op("scalar", "ac", lambda s, sub=sub: s.activation(
                rstd[:, sub:sub + 1], mv[:, 2 * sub + 1:2 * sub + 2], AF.Sqrt,
                bias=epst[:, 0:1]))
            p.wait("vector", "ac", av)
            p.op("vector", "dv", lambda v, sub=sub: v.reciprocal(
                rstd2[:, sub:sub + 1], rstd[:, sub:sub + 1]))
            p.wait("vector", "pe", xtk_free_pe)
            dvv = p.op("vector", "dv", lambda v, sl=sl, sub=sub: v.tensor_scalar(
                xtk[:], x_f[sl][:], mv[:, 2 * sub:2 * sub + 1], rstd2[:, sub:sub + 1],
                ALU.subtract, ALU.mult))
            xf_free_dv[sl] = dvv
            if copy_acc:
                p.wait("gpsimd", ldsem, lv)
                p.wait("gpsimd", "pe", acc_free_pe[0])
                plv = p.op("gpsimd", "pl", lambda q, sl=sl, acc_off=acc_off:
                           q.tensor_copy(acc[:, acc_off:acc_off + D], x_f[sl][:]))
                xf_free_pl[sl] = plv
            p.wait("tensor", "dv", dvv)
            p.wait("tensor", "ac", ps_free_ac[par])
            p.wait("tensor", "dv", ps_free_dv[par])
            for c in range(16):
                p.mm(lambda t, c=c, par=par: t.matmul(
                    psX[par][:, c * 128:(c + 1) * 128], xtk[:, c * 128:(c + 1) * 128],
                    eyeB[:], start=True, stop=True), inc=1 if c == 15 else 0)
            pv = p.c["pe"]
            xtk_free_pe = pv
            p.wait("scalar", "pe", pv)
            av = p.op("scalar", "ac", lambda s, par=par, sub=sub: s.copy(
                xnT[:].rearrange("p (c s) -> p c s", c=EC)[:, :, sub * 128:(sub + 1) * 128],
                psX[par][:].rearrange("p (c s) -> p c s", c=EC)))
            ps_free_ac[par] = av

        first_strip_wait_done = False
        carry_par = 0
        for pgi in range(G):
            for sub in range(NSUB):
                ln_sub(xprev[(pgi * NSUB + sub) * 128:(pgi * NSUB + sub + 1) * 128, :],
                       sub, False, 0, "ld0")
            for ec in range(EC):
                if not first_strip_wait_done:
                    p.wait("sync", "gp0", gp0_all_done)
                    first_strip_wait_done = True
                s1, s2 = (2 * ec) % 3, (2 * ec + 1) % 3
                for (sl2, pi2) in ((s1, 1), (s2, 2)):
                    p.wait("sync", "pe", rql_free_pe[sl2])
                    p.wait("sync", "gp0", rql_free_gp[sl2])
                    rql_free_gp[sl2] = 0
                    lv = p.dma("sync", "ld0", rql[sl2][:],
                               s_wqkv[pi2, ec].rearrange("p c n -> p (c n)"))
                p.wait("tensor", "ld0", lv)
                p.wait("tensor", "dv", ps_free_dv[0])
                p.wait("tensor", "ac", ps_free_ac[0])
                p.wait("tensor", "ac", ps_free_ac[1])
                for (sl2, off) in ((s1, 0), (s2, 1024)):
                    for d in range(16):
                        for h2 in range(2):
                            p.mm(lambda t, sl2=sl2, off=off, d=d, h2=h2: t.matmul(
                                psA[:, off + h2 * 512: off + (h2 + 1) * 512],
                                rql[sl2][:, d * 128:(d + 1) * 128],
                                xnT[:, d * GT + h2 * 512: d * GT + (h2 + 1) * 512],
                                start=(d == 0), stop=(d == 15)),
                                inc=1 if (d == 15 and h2 == 1) else 0)
                    rql_free_pe[sl2] = p.c["pe"]
                pv = p.c["pe"]
                p.wait("scalar", "pe", pv)
                avk = p.op("scalar", "ac", lambda s: s.copy(ktmp[:], psA[:, 0:1024]))
                p.wait("vector", "pe", pv)
                p.wait("vector", "ac", avk)
                p.op("vector", "dv", lambda v: v.tensor_tensor(
                    kvb[:], psA[:, 1024:2048], ktmp[:], ALU.mult))
                p.op("vector", "dv", lambda v: v.tensor_reduce(tmp1[:], kvb[:], AX, ALU.add))
                cs, cd = (carA, carB) if carry_par == 0 else (carB, carA)
                dvv = p.op("vector", "dv", lambda v, cs=cs, cd=cd, ec=ec: v.tensor_tensor(
                    cd[:, ec:ec + 1], cs[:, ec:ec + 1], tmp1[:], ALU.add))
                ps_free_dv[0] = dvv
                ps_free_ac[0] = avk
            carry_par = 1 - carry_par
        carry = carA if carry_par == 0 else carB

        vh_done = [0] * EC
        vh_prev_last = 0
        m2_evac_q = 0
        m2_evac_k = 0
        m2_evac_kv = 0
        for g in range(G):
            for sub in range(NSUB):
                ln_sub(xin[(g * NSUB + sub) * 128:(g * NSUB + sub + 1) * 128, :],
                       sub, True, sub * D, "ldm")
            acc_pl_done = p.c["pl"]
            for ec in range(EC):
                sl_q, sl_k, sl_v = (3 * ec) % 3, (3 * ec + 1) % 3, (3 * ec + 2) % 3
                for (sl2, pi2) in ((sl_q, 0), (sl_k, 1), (sl_v, 2)):
                    p.wait("sync", "pe", rql_free_pe[sl2])
                    p.dma("sync", "ldm", rql[sl2][:],
                          s_wqkv[pi2, ec].rearrange("p c n -> p (c n)"))
                lv = p.c["ldm"]
                p.wait("tensor", "ldm", lv - 32)
                p.wait("tensor", "ac", m2_evac_q if (ec or g) else ps_free_ac[0])
                if ec == 0:
                    p.wait("tensor", "ac", ps_free_ac[0])
                    p.wait("tensor", "ac", ps_free_ac[1])
                    p.wait("tensor", "dv", ps_free_dv[0])
                for d in range(16):
                    for h2 in range(2):
                        p.mm(lambda t, sl2=sl_q, d=d, h2=h2: t.matmul(
                            psA[:, h2 * 512:(h2 + 1) * 512],
                            rql[sl2][:, d * 128:(d + 1) * 128],
                            xnT[:, d * GT + h2 * 512: d * GT + (h2 + 1) * 512],
                            start=(d == 0), stop=(d == 15)),
                            inc=1 if (d == 15 and h2 == 1) else 0)
                pq = p.c["pe"]
                rql_free_pe[sl_q] = pq
                p.wait("tensor", "ldm", lv - 16)
                p.wait("tensor", "ac", m2_evac_k)
                for d in range(16):
                    for h2 in range(2):
                        p.mm(lambda t, sl2=sl_k, d=d, h2=h2: t.matmul(
                            psA[:, 1024 + h2 * 512: 1024 + (h2 + 1) * 512],
                            rql[sl2][:, d * 128:(d + 1) * 128],
                            xnT[:, d * GT + h2 * 512: d * GT + (h2 + 1) * 512],
                            start=(d == 0), stop=(d == 15)),
                            inc=1 if (d == 15 and h2 == 1) else 0)
                pk = p.c["pe"]
                rql_free_pe[sl_k] = pk
                p.wait("tensor", "ldm", lv)
                p.wait("tensor", "dv", m2_evac_kv)
                for d in range(16):
                    for h2 in range(2):
                        p.mm(lambda t, sl2=sl_v, d=d, h2=h2: t.matmul(
                            psB[:, h2 * 512:(h2 + 1) * 512],
                            rql[sl2][:, d * 128:(d + 1) * 128],
                            xnT[:, d * GT + h2 * 512: d * GT + (h2 + 1) * 512],
                            start=(d == 0), stop=(d == 15)),
                            inc=1 if (d == 15 and h2 == 1) else 0)
                pv_ = p.c["pe"]
                rql_free_pe[sl_v] = pv_
                p.wait("scalar", "pe", pq)
                p.wait("scalar", "dv", vh_done[ec - 1] if ec else vh_prev_last)
                m2_evac_q = p.op("scalar", "ac", lambda s: s.activation(
                    qtmp[:], psA[:, 0:1024], AF.Copy, scale=DENOM))
                p.wait("scalar", "pe", pk)
                p.wait("scalar", "dv", m2_evac_kv)
                m2_evac_k = p.op("scalar", "ac", lambda s: s.copy(ktmp[:], psA[:, 1024:2048]))
                p.wait("vector", "pe", pv_)
                p.wait("vector", "ac", m2_evac_k)
                m2_evac_kv = p.op("vector", "dv", lambda v: v.tensor_tensor(
                    kvb[:], psB[:, 0:1024], ktmp[:], ALU.mult))
                p.op("vector", "dv", lambda v, ec=ec, carry=carry: v.tensor_tensor_scan(
                    kvt[:], sones[:], kvb[:], carry[:, ec:ec + 1], ALU.mult, ALU.add))
                p.op("vector", "dv", lambda v, ec=ec, carry=carry: v.tensor_copy(
                    carry[:, ec:ec + 1], kvt[:, GT - 1:GT]))
                p.wait("vector", "ac", m2_evac_q)
                vh_done[ec] = p.op("vector", "dv", lambda v, ec=ec: v.tensor_tensor(
                    vhat[:, ec * GT:(ec + 1) * GT], kvt[:], qtmp[:], ALU.mult))
            vh_prev_last = vh_done[EC - 1]
            m3_evac = [0] * NSUB
            for fq in range(FQ):
                p.wait("tensor", "pl", acc_pl_done)
                if fq == 0:
                    p.wait("tensor", "ac", m2_evac_k)
                    p.wait("tensor", "dv", m2_evac_kv)
                else:
                    for sub in range(NSUB):
                        p.wait("tensor", "ac", m3_evac[sub])
                for sub in range(NSUB):
                    ps = psA if sub < 4 else psB
                    po = (sub % 4) * 512
                    p.mm(lambda t, sub=sub, fq=fq, ps=ps, po=po: t.matmul(
                        ps[:, po:po + 512], eyeB[:],
                        acc[:, sub * D + fq * 512: sub * D + (fq + 1) * 512],
                        start=True, stop=False))
                for ec in range(EC):
                    sl2 = (fq * EC + ec) % 3
                    p.wait("sync", "pe", rwo_free_pe[sl2])
                    lv = p.dma("sync", "ldm", rwo[sl2][:], s_wo[fq, ec])
                    p.wait("tensor", "ldm", lv)
                    p.wait("tensor", "dv", vh_done[ec])
                    for sub in range(NSUB):
                        ps = psA if sub < 4 else psB
                        po = (sub % 4) * 512
                        p.mm(lambda t, sub=sub, ec=ec, sl2=sl2, ps=ps, po=po: t.matmul(
                            ps[:, po:po + 512],
                            vhat[:, ec * GT + sub * 128: ec * GT + (sub + 1) * 128],
                            rwo[sl2][:], start=False, stop=(ec == EC - 1)),
                            inc=1 if (sub == NSUB - 1 or ec == EC - 1) else 0)
                    rwo_free_pe[sl2] = p.c["pe"]
                pe_end3 = p.c["pe"]
                for sub in range(NSUB):
                    ps = psA if sub < 4 else psB
                    po = (sub % 4) * 512
                    p.wait("scalar", "pe", pe_end3 - NSUB + sub + 1)
                    m3_evac[sub] = p.op("scalar", "ac", lambda s, sub=sub, fq=fq, ps=ps, po=po:
                                        s.copy(acc[:, sub * D + fq * 512: sub * D + (fq + 1) * 512],
                                               ps[:, po:po + 512]))
            hid_done = [0] * HC
            sig_prev = [0, 0]
            for hc in range(HC):
                par = hc % 2
                ps = psX[par]
                sl_a, sl_b = (2 * hc) % 3, (2 * hc + 1) % 3
                for (sl2, sm) in ((sl_a, s_w1), (sl_b, s_w2)):
                    p.wait("sync", "pe", rql_free_pe[sl2])
                    p.dma("sync", "ldm", rql[sl2][:], sm[hc].rearrange("p c n -> p (c n)"))
                lv = p.c["ldm"]
                p.wait("tensor", "ldm", lv)
                if hc < 2:
                    for sub in range(NSUB):
                        p.wait("tensor", "ac", m3_evac[sub])
                else:
                    p.wait("tensor", "ac", sig_prev[par])
                    p.wait("tensor", "dv", hid_done[hc - 2])
                for d in range(16):
                    for h2 in range(2):
                        p.mm(lambda t, sl2=sl_a, d=d, h2=h2, ps=ps: t.matmul(
                            ps[:, h2 * 512:(h2 + 1) * 512],
                            rql[sl2][:, d * 128:(d + 1) * 128],
                            xnT[:, d * GT + h2 * 512: d * GT + (h2 + 1) * 512],
                            start=(d == 0), stop=(d == 15)),
                            inc=1 if (d == 15 and h2 == 1) else 0)
                p1 = p.c["pe"]
                for d in range(16):
                    for h2 in range(2):
                        p.mm(lambda t, sl2=sl_b, d=d, h2=h2, ps=ps: t.matmul(
                            ps[:, 1024 + h2 * 512: 1024 + (h2 + 1) * 512],
                            rql[sl2][:, d * 128:(d + 1) * 128],
                            xnT[:, d * GT + h2 * 512: d * GT + (h2 + 1) * 512],
                            start=(d == 0), stop=(d == 15)),
                            inc=1 if (d == 15 and h2 == 1) else 0)
                p2 = p.c["pe"]
                rql_free_pe[sl_a] = p1
                rql_free_pe[sl_b] = p2
                p.wait("scalar", "pe", p1)
                if hc:
                    p.wait("scalar", "dv", hid_done[hc - 1])
                sv = p.op("scalar", "ac", lambda s, ps=ps: s.activation(
                    sgt[:], ps[:, 0:1024], AF.Sigmoid))
                sig_prev[par] = sv
                p.wait("vector", "pe", p2)
                p.wait("vector", "ac", sv)
                if hc == 0:
                    p.wait("vector", "pe", pe_end3)
                hid_done[hc] = p.op("vector", "dv", lambda v, hc=hc, ps=ps: v.tensor_tensor(
                    hidv[:, hc * GT:(hc + 1) * GT], ps[:, 1024:2048], sgt[:], ALU.mult))
            m5_evac = [0] * NSUB
            for fq in range(FQ):
                if fq == 0:
                    p.wait("tensor", "ac", sig_prev[0])
                    p.wait("tensor", "ac", sig_prev[1])
                    p.wait("tensor", "dv", hid_done[HC - 1])
                    p.wait("tensor", "dv", hid_done[HC - 2])
                else:
                    for sub in range(NSUB):
                        p.wait("tensor", "ac", m5_evac[sub])
                for sub in range(NSUB):
                    ps = psA if sub < 4 else psB
                    po = (sub % 4) * 512
                    p.mm(lambda t, sub=sub, fq=fq, ps=ps, po=po: t.matmul(
                        ps[:, po:po + 512], eyeB[:],
                        acc[:, sub * D + fq * 512: sub * D + (fq + 1) * 512],
                        start=True, stop=False))
                for hc in range(HC):
                    sl2 = (fq * HC + hc) % 3
                    p.wait("sync", "pe", rwo_free_pe[sl2])
                    lv = p.dma("sync", "ldm", rwo[sl2][:], s_w3[fq, hc])
                    p.wait("tensor", "ldm", lv)
                    if fq == 0:
                        p.wait("tensor", "dv", hid_done[hc])
                    for sub in range(NSUB):
                        ps = psA if sub < 4 else psB
                        po = (sub % 4) * 512
                        p.mm(lambda t, sub=sub, hc=hc, sl2=sl2, ps=ps, po=po: t.matmul(
                            ps[:, po:po + 512],
                            hidv[:, hc * GT + sub * 128: hc * GT + (sub + 1) * 128],
                            rwo[sl2][:], start=False, stop=(hc == HC - 1)),
                            inc=1 if (sub == NSUB - 1 or hc == HC - 1) else 0)
                    rwo_free_pe[sl2] = p.c["pe"]
                pe_end5 = p.c["pe"]
                for sub in range(NSUB):
                    ps = psA if sub < 4 else psB
                    po = (sub % 4) * 512
                    osl = (fq * NSUB + sub) % 2
                    p.wait("scalar", "pe", pe_end5 - NSUB + sub + 1)
                    p.wait("scalar", "gpm", outst_free_gp[osl])
                    av = p.op("scalar", "ac", lambda s, ps=ps, po=po, osl=osl: s.copy(
                        outst[osl][:], ps[:, po:po + 512]))
                    m5_evac[sub] = av
                    p.wait("gpsimd", "ac", av)
                    r0 = (g * NSUB + sub) * 128
                    gv = p.dma("gpsimd", "gpm",
                               out[r0:r0 + 128, fq * 512:(fq + 1) * 512], outst[osl][:])
                    outst_free_gp[osl] = gv
            m2_evac_q = m5_evac[3]
            m2_evac_k = m5_evac[3]
            m2_evac_kv = 0
            ps_free_ac = [m5_evac[3], m5_evac[7]]
            ps_free_dv = [0, 0]
            acc_free_pe[0] = p.c["pe"]

        p.wait("gpsimd", "gpm", p.c["gpm"])
        p.wait("gpsimd", "gp0", p.c["gp0"])

        with nc.Block() as block:
            @block.sync
            def _(q):
                for fn in p.ops["sync"]:
                    fn(q)

            @block.tensor
            def _(q):
                for fn in p.ops["tensor"]:
                    fn(q)

            @block.vector
            def _(q):
                for fn in p.ops["vector"]:
                    fn(q)

            @block.scalar
            def _(q):
                for fn in p.ops["scalar"]:
                    fn(q)

            @block.gpsimd
            def _(q):
                for fn in p.ops["gpsimd"]:
                    fn(q)

    return nc


_NC_CACHE = [None]


def kernel(L0_attn_x, L0_attn_Wq, L0_attn_Wk, L0_attn_Wv, L0_attn_Wo,
           L0_ffn_w1, L0_ffn_w2, L0_ffn_w3):
    x = np.ascontiguousarray(np.asarray(L0_attn_x, dtype=np.float32))
    ws = {
        "wq": np.ascontiguousarray(np.asarray(L0_attn_Wq, dtype=np.float32)),
        "wk": np.ascontiguousarray(np.asarray(L0_attn_Wk, dtype=np.float32)),
        "wv": np.ascontiguousarray(np.asarray(L0_attn_Wv, dtype=np.float32)),
        "wo": np.ascontiguousarray(np.asarray(L0_attn_Wo, dtype=np.float32)),
        "w1": np.ascontiguousarray(np.asarray(L0_ffn_w1, dtype=np.float32)),
        "w2": np.ascontiguousarray(np.asarray(L0_ffn_w2, dtype=np.float32)),
        "w3": np.ascontiguousarray(np.asarray(L0_ffn_w3, dtype=np.float32)),
        "eye_f": np.eye(128, dtype=np.float32),
        "eye_bi": np.eye(128, dtype=ml_dtypes.bfloat16),
        "ones_i": np.ones((1, 128), dtype=np.float32),
    }
    zeros = np.zeros((T, D), dtype=np.float32)
    in_maps = []
    for c in range(8):
        b, h = c // 2, c % 2
        m = dict(ws)
        m["xin"] = np.ascontiguousarray(x[b, h * T:(h + 1) * T, :])
        m["xprev"] = np.ascontiguousarray(x[b, 0:T, :]) if h == 1 else zeros
        in_maps.append(m)

    if _NC_CACHE[0] is None:
        _NC_CACHE[0] = build_nc()
    nc = _NC_CACHE[0]
    res = run_bass_kernel_spmd(nc, in_maps, core_ids=list(range(8)))
    outp = np.empty((NB, SEQ, D), dtype=np.float32)
    for c in range(8):
        b, h = c // 2, c % 2
        outp[b, h * T:(h + 1) * T, :] = res.results[c]["out"]
    return outp


# version 07
# speedup vs baseline: 1.1792x; 1.1792x over previous
import numpy as np
import ml_dtypes
from contextlib import ExitStack

import concourse.bass as bass
import concourse.mybir as mybir
from concourse.bass_utils import run_bass_kernel_spmd

F32 = mybir.dt.float32
BF16 = mybir.dt.bfloat16
AF = mybir.ActivationFunctionType
ALU = mybir.AluOpType
AX = mybir.AxisListType.X

D = 2048
HH = 5632
NB = 4
SEQ = 8192
T = 4096
G = 4
GT = 1024
NSUB = 8
EC = 16
HC = 44
FQ = 4
DENOM = float(np.float32(np.sqrt(np.float32(2048.0))) / np.float32(8192.0))
N_EL = float(HH * D)

ENGS = ("sync", "tensor", "vector", "scalar", "gpsimd")


class Prog:
    def __init__(self):
        self.ops = {e: [] for e in ENGS}
        self.c = {}
        self.sems = {}

    def sem(self, name):
        self.c.setdefault(name, 0)
        return name

    def raw(self, eng, fn):
        self.ops[eng].append(fn)

    def wait(self, eng, sem, val):
        if not val or val <= 0:
            return
        def f(q, sem=sem, val=val):
            q.wait_ge(self.sems[sem], val)
        self.raw(eng, f)

    def dma(self, eng, sem, out, in_):
        self.sem(sem)
        self.c[sem] += 16
        def f(q, out=out, in_=in_, sem=sem):
            q.dma_start(out=out, in_=in_).then_inc(self.sems[sem], 16)
        self.raw(eng, f)
        return self.c[sem]

    def op(self, eng, sem, fn):
        self.sem(sem)
        prev = self.c[sem]
        if prev:
            self.wait(eng, sem, prev)
        self.c[sem] += 1
        def f(q, fn=fn, sem=sem):
            fn(q).then_inc(self.sems[sem], 1)
        self.raw(eng, f)
        return self.c[sem]

    def mm(self, fn, inc=0):
        self.sem("pe")
        if inc:
            self.c["pe"] += inc
        def f(q, fn=fn, inc=inc):
            ins = fn(q)
            if inc:
                ins.then_inc(self.sems["pe"], inc)
        self.raw("tensor", f)
        return self.c["pe"]


def build_nc():
    nc = bass.Bass("TRN2", target_bir_lowering=False)

    xin = nc.dram_tensor("xin", [T, D], F32, kind="ExternalInput").ap()
    xprev = nc.dram_tensor("xprev", [T, D], F32, kind="ExternalInput").ap()
    wq_b = nc.dram_tensor("wq_b", [EC, 128, D], F32, kind="ExternalInput").ap()
    wk_b = nc.dram_tensor("wk_b", [EC, 128, D], F32, kind="ExternalInput").ap()
    wv_b = nc.dram_tensor("wv_b", [EC, 128, D], F32, kind="ExternalInput").ap()
    wo_b = nc.dram_tensor("wo_b", [EC, 128, D], F32, kind="ExternalInput").ap()
    w1_b = nc.dram_tensor("w1_b", [HC, 128, D], F32, kind="ExternalInput").ap()
    w2_b = nc.dram_tensor("w2_b", [HC, 128, D], F32, kind="ExternalInput").ap()
    w3_b = nc.dram_tensor("w3_b", [HC, 128, D], F32, kind="ExternalInput").ap()
    eye_bi = nc.dram_tensor("eye_bi", [128, 128], BF16, kind="ExternalInput").ap()
    ones_i = nc.dram_tensor("ones_i", [1, 128], F32, kind="ExternalInput").ap()
    out = nc.dram_tensor("out", [T, D], F32, kind="ExternalOutput").ap()

    s_wqkv = nc.dram_tensor("s_wqkv", [3, EC, 128, D], BF16, kind="Internal").ap()
    s_wo = nc.dram_tensor("s_wo", [FQ, EC, 128, 512], BF16, kind="Internal").ap()
    s_w1 = nc.dram_tensor("s_w1", [HC, 128, D], BF16, kind="Internal").ap()
    s_w2 = nc.dram_tensor("s_w2", [HC, 128, D], BF16, kind="Internal").ap()
    s_w3 = nc.dram_tensor("s_w3", [FQ, HC, 128, 512], BF16, kind="Internal").ap()
    s_red = nc.dram_tensor("s_red", [3, 128], F32, kind="Internal").ap()

    p = Prog()

    with ExitStack() as ctx:
        e = ctx.enter_context

        xnT = e(nc.sbuf_tensor("xnT", [128, EC * GT], BF16))
        hidv = e(nc.sbuf_tensor("hidv", [128, HC * GT], BF16))
        acc = e(nc.sbuf_tensor("acc", [128, NSUB * D], BF16))
        x_f = [e(nc.sbuf_tensor(f"x_f{i}", [128, D], F32)) for i in range(2)]
        xtk = e(nc.sbuf_tensor("xtk", [128, D], BF16))
        rql = [e(nc.sbuf_tensor(f"rql{i}", [128, D], BF16)) for i in range(3)]
        rwo = [e(nc.sbuf_tensor(f"rwo{i}", [128, 512], BF16)) for i in range(3)]
        outst = [e(nc.sbuf_tensor(f"outst{i}", [128, 512], F32)) for i in range(2)]
        qtmp = e(nc.sbuf_tensor("qtmp", [128, GT], BF16))
        ktmp = e(nc.sbuf_tensor("ktmp", [128, GT], BF16))
        kvb = e(nc.sbuf_tensor("kvb", [128, GT], BF16))
        kvt = e(nc.sbuf_tensor("kvt", [128, GT], BF16))
        sgt = e(nc.sbuf_tensor("sgt", [128, GT], BF16))
        eyeB = e(nc.sbuf_tensor("eyeB", [128, 128], BF16))
        onesrow = e(nc.sbuf_tensor("onesrow", [1, 128], F32))
        sones = e(nc.sbuf_tensor("sones", [128, GT], BF16))
        carA = e(nc.sbuf_tensor("carA", [128, EC], F32))
        carB = e(nc.sbuf_tensor("carB", [128, EC], F32))
        tall = e(nc.sbuf_tensor("tall", [128, 4], F32))
        tneg = e(nc.sbuf_tensor("tneg", [128, 4], F32))
        epst = e(nc.sbuf_tensor("epst", [128, 1], F32))
        stats = e(nc.sbuf_tensor("stats", [128, 24], F32))
        mv = e(nc.sbuf_tensor("mv", [128, 2 * NSUB], F32))
        rstd = e(nc.sbuf_tensor("rstd", [128, NSUB], F32))
        rstd2 = e(nc.sbuf_tensor("rstd2", [128, NSUB], F32))
        pg = e(nc.sbuf_tensor("pg", [128, 136], F32))
        pgs = e(nc.sbuf_tensor("pgs", [128, 1], F32))
        redrow = e(nc.sbuf_tensor("redrow", [1, 128], F32))
        tsc = e(nc.sbuf_tensor("tsc", [1, 4], F32))
        tmp1 = e(nc.sbuf_tensor("tmp1", [128, 1], F32))

        psA = e(nc.psum_tensor("psA", [128, 2048], F32))
        psB = e(nc.psum_tensor("psB", [128, 2048], F32))
        psX = [psA, psB]
        vhat = hidv
        af = [hidv[:, 6144:10240].bitcast(F32), hidv[:, 10240:14336].bitcast(F32)]
        ast = [hidv[:, 0:2048], hidv[:, 2048:4096], hidv[:, 4096:6144]]
        cf = [acc[:, 6144:10240].bitcast(F32), acc[:, 10240:14336].bitcast(F32)]

        for s in ("ld0", "ldm", "lds", "ldx", "gp0", "gpm", "pe", "dv", "ac"):
            p.sems[s] = e(nc.semaphore(s))
            p.sem(s)

        p.dma("sync", "ld0", eyeB[:], eye_bi)
        p.dma("sync", "ld0", onesrow[:], ones_i)
        p.wait("vector", "ld0", 32)
        p.op("vector", "dv", lambda v: v.memset(epst[:], 1e-5))
        p.op("vector", "dv", lambda v: v.memset(sones[:], 1.0))
        p.op("vector", "dv", lambda v: v.memset(carA[:], 0.0))
        p.op("vector", "dv", lambda v: v.memset(carB[:], 0.0))

        af_free_ac = [0, 0]
        ast_free_gp = [0, 0, 0]
        cf_free_dv = [0, 0]
        cstg_free = [0, 0]
        xf_free_dv = [0, 0]
        xf_free_pl = [0, 0]
        rql_free_pe = [0, 0, 0]
        rwo_free_pe = [0, 0, 0]
        outst_free_gp = [0, 0]
        ps_free_ac = [0, 0]
        ps_free_dv = [0, 0]
        acc_free_dv = [0] * NSUB
        acc_copy_ac = [0] * NSUB
        xtk_free_pe = [0]
        bc_free_ac = [0]
        store_done = {}

        a_jobs = []
        for ec in range(EC):
            a_jobs.append((wk_b[ec], 1, ec))
            a_jobs.append((wv_b[ec], 2, ec))
        for ec in range(EC):
            a_jobs.append((wq_b[ec], 0, ec))
        for ec in range(EC):
            a_jobs.append((wo_b[ec], 3, ec))
        a_next = [0]

        def emit_a(n):
            while n > 0 and a_next[0] < len(a_jobs):
                src, pi, ec = a_jobs[a_next[0]]
                idx = a_next[0]
                a_next[0] += 1
                n -= 1
                sl = idx % 2
                stg = idx % 3
                p.wait("sync", "ac", af_free_ac[sl])
                lv = p.dma("sync", "ld0", af[sl][:], src)
                p.wait("scalar", "ld0", lv)
                p.wait("scalar", "gp0", ast_free_gp[stg])
                av = p.op("scalar", "ac", lambda s, sl=sl, stg=stg: s.copy(
                    ast[stg][:], af[sl][:]))
                af_free_ac[sl] = av
                p.wait("gpsimd", "ac", av)
                if pi < 3:
                    gv = p.dma("gpsimd", "gp0", s_wqkv[pi, ec], ast[stg][:])
                    store_done[(pi, ec)] = gv
                else:
                    for fq in range(FQ):
                        gv = p.dma("gpsimd", "gp0", s_wo[fq, ec],
                                   ast[stg][:, fq * 512:(fq + 1) * 512])
                    store_done[("wo", ec)] = gv
                ast_free_gp[stg] = gv

        gamma_jobs = []
        for mi, W in enumerate((w1_b, w2_b, w3_b)):
            for hc in range(HC):
                gamma_jobs.append((W[hc], mi * HC + hc))
        g_next = [0]

        def emit_gamma(n):
            while n > 0 and g_next[0] < len(gamma_jobs):
                src, gcol = gamma_jobs[g_next[0]]
                g_next[0] += 1
                n -= 1
                sl = gcol % 2
                p.wait("scalar", "dv", cf_free_dv[sl])
                lv = p.dma("scalar", "lds", cf[sl][:], src)
                p.wait("vector", "lds", lv)
                dvv = p.op("vector", "dv", lambda v, sl=sl, gcol=gcol:
                           v.tensor_reduce(pg[:, gcol:gcol + 1], cf[sl][:],
                                           AX, ALU.add, apply_absolute_value=True))
                cf_free_dv[sl] = dvv

        fin_state = {}

        def emit_fin_a(mi):
            c0, c1 = mi * HC, (mi + 1) * HC
            if mi:
                p.wait("vector", "gpm", fin_state[mi - 1][0])
            p.op("vector", "dv", lambda v, c0=c0, c1=c1: v.tensor_reduce(
                pgs[:], pg[:, c0:c1], AX, ALU.add))
            p.wait("gpsimd", "dv", p.c["dv"])
            gv = p.dma("gpsimd", "gpm", s_red[mi].rearrange("(pp u) -> pp u", u=1), pgs[:])
            p.wait("sync", "gpm", gv)
            if mi:
                p.wait("sync", "dv", fin_state[mi - 1][1])
            lv = p.dma("sync", "ld0", redrow[:], s_red[mi].rearrange("(u pp) -> u pp", u=1))
            p.wait("vector", "ld0", lv)
            p.op("vector", "dv", lambda v, mi=mi: v.tensor_reduce(
                tsc[:, mi:mi + 1], redrow[:], AX, ALU.add))
            rdone = p.c["dv"]
            p.op("vector", "dv", lambda v, mi=mi: v.tensor_scalar(
                tsc[:, mi:mi + 1], tsc[:, mi:mi + 1], 0.5 / N_EL, 0.5e-8, ALU.mult, ALU.add))
            fin_state[mi] = (gv, rdone, p.c["dv"])

        def emit_fin_b(mi):
            p.wait("tensor", "dv", fin_state[mi][2])
            p.wait("tensor", "ac", bc_free_ac[0])
            p.mm(lambda t, mi=mi: t.matmul(psB[:, 0:1], onesrow[:], tsc[0:1, mi:mi + 1],
                                           start=True, stop=True), inc=1)
            pv = p.c["pe"]
            p.wait("scalar", "pe", pv)
            av = p.op("scalar", "ac", lambda s, mi=mi: s.copy(tall[:, mi:mi + 1], psB[:, 0:1]))
            bc_free_ac[0] = av
            p.wait("vector", "ac", av)
            p.op("vector", "dv", lambda v, mi=mi: v.tensor_scalar(
                tneg[:, mi:mi + 1], tall[:, mi:mi + 1], -1.0, None, ALU.mult))

        c_jobs = []
        for mi, (W, dsts) in enumerate([
            (w1_b, None), (w2_b, None), (w3_b, True),
        ]):
            for hc in range(HC):
                if mi == 0:
                    dl = [(s_w1[hc], 0, D)]
                elif mi == 1:
                    dl = [(s_w2[hc], 0, D)]
                else:
                    dl = [(s_w3[fq, hc], fq * 512, 512) for fq in range(FQ)]
                c_jobs.append((W[hc], mi, dl))
        c_next = [0]
        c_marks = {}

        def emit_c(n):
            while n > 0 and c_next[0] < len(c_jobs):
                src, m, dl = c_jobs[c_next[0]]
                cidx = c_next[0]
                c_next[0] += 1
                n -= 1
                sl = cidx % 2
                stg = cidx % 2
                p.wait("scalar", "dv", cf_free_dv[sl])
                lv = p.dma("scalar", "lds", cf[sl][:], src)
                p.wait("vector", "lds", lv)
                p.wait("vector", "gp0", cstg_free[stg])
                soff = 2048 * (1 + stg)
                p.op("vector", "dv", lambda v, sl=sl, m=m: v.tensor_scalar(
                    acc[:, 0:2048], cf[sl][:], tneg[:, m:m + 1], None, ALU.is_lt))
                dvv = p.op("vector", "dv", lambda v, sl=sl, m=m, soff=soff:
                           v.scalar_tensor_tensor(acc[:, soff:soff + 2048], cf[sl][:],
                                                  tall[:, m:m + 1], acc[:, 0:2048],
                                                  ALU.is_gt, ALU.subtract))
                cf_free_dv[sl] = dvv
                p.wait("gpsimd", "dv", dvv)
                for (dst, coff, w) in dl:
                    gv = p.dma("gpsimd", "gp0", dst,
                               acc[:, soff + coff: soff + coff + w])
                cstg_free[stg] = gv
                if cidx == 2 * HC - 1:
                    c_marks["w12"] = p.c["gp0"]

        def ln_sub(src_rows, sub, copy_acc, acc_off, eng, ldsem, par):
            sl = sub % 2
            p.wait(eng, "dv", xf_free_dv[sl])
            p.wait(eng, "ac", xf_free_pl[sl])
            lv = p.dma(eng, ldsem, x_f[sl][:], src_rows)
            p.wait("vector", ldsem, lv)
            for j in range(4):
                p.op("vector", "dv", lambda v, sl=sl, j=j: v.bn_stats(
                    stats[:, j * 6:(j + 1) * 6], x_f[sl][:, j * 512:(j + 1) * 512]))
            p.op("vector", "dv", lambda v, sub=sub: v.bn_aggr(
                mv[:, 2 * sub:2 * sub + 2], stats[:]))
            p.wait("scalar", "dv", p.c["dv"])
            av = p.op("scalar", "ac", lambda s, sub=sub: s.activation(
                rstd[:, sub:sub + 1], mv[:, 2 * sub + 1:2 * sub + 2], AF.Sqrt,
                bias=epst[:, 0:1]))
            if copy_acc:
                p.wait("scalar", ldsem, lv)
                p.wait("scalar", "dv", acc_free_dv[sub])
                plv = p.op("scalar", "ac", lambda s, sl=sl, acc_off=acc_off:
                           s.copy(acc[:, acc_off:acc_off + D], x_f[sl][:]))
                xf_free_pl[sl] = plv
                acc_copy_ac[sub] = plv
            p.wait("vector", "ac", av)
            p.op("vector", "dv", lambda v, sub=sub: v.reciprocal(
                rstd2[:, sub:sub + 1], rstd[:, sub:sub + 1]))
            p.wait("vector", "pe", xtk_free_pe[0])
            dvv = p.op("vector", "dv", lambda v, sl=sl, sub=sub: v.tensor_scalar(
                xtk[:], x_f[sl][:], mv[:, 2 * sub:2 * sub + 1], rstd2[:, sub:sub + 1],
                ALU.subtract, ALU.mult))
            xf_free_dv[sl] = dvv
            p.wait("tensor", "dv", dvv)
            p.wait("tensor", "ac", ps_free_ac[par])
            p.wait("tensor", "dv", ps_free_dv[par])
            for c in range(16):
                p.mm(lambda t, c=c, par=par: t.matmul(
                    psX[par][:, c * 128:(c + 1) * 128], xtk[:, c * 128:(c + 1) * 128],
                    eyeB[:], start=True, stop=True), inc=1 if c == 15 else 0)
            pv = p.c["pe"]
            xtk_free_pe[0] = pv
            p.wait("scalar", "pe", pv)
            av2 = p.op("scalar", "ac", lambda s, par=par, sub=sub: s.copy(
                xnT[:].rearrange("p (c s) -> p c s", c=EC)[:, :, sub * 128:(sub + 1) * 128],
                psX[par][:].rearrange("p (c s) -> p c s", c=EC)))
            ps_free_ac[par] = av2

        emit_a(4)
        for sub in range(NSUB):
            ln_sub(xprev[sub * 128:(sub + 1) * 128, :], sub, False, 0, "sync", "ld0", 0)
            emit_a(1)
        carry_par = 0
        for pgi in range(G):
            if pgi > 0:
                for sub in range(NSUB):
                    ln_sub(xprev[(pgi * NSUB + sub) * 128:(pgi * NSUB + sub + 1) * 128, :],
                           sub, False, 0, "sync", "ld0", 0)
            if pgi == 1:
                emit_a(len(a_jobs))
            for ec in range(EC):
                if pgi == 0:
                    want = 2 * (ec + 1) + max(0, 2 * (ec - 7))
                    emit_a(max(0, want - a_next[0]))
                    emit_gamma((ec + 1) * HC // EC - g_next[0])
                elif pgi == 1:
                    emit_gamma(HC + (ec + 1) * HC // EC - g_next[0])
                    if ec == 2:
                        emit_fin_b(0)
                    if ec >= 3:
                        emit_c(min(4, HC - c_next[0] + 0) if c_next[0] < HC else 0)
                elif pgi == 2:
                    emit_gamma(2 * HC + (ec + 1) * HC // EC - g_next[0])
                    if ec == 2:
                        emit_fin_b(1)
                    if ec >= 3 and c_next[0] < 2 * HC:
                        emit_c(min(4, 2 * HC - c_next[0]))
                else:
                    if ec == 2:
                        emit_fin_b(2)
                    if ec >= 3:
                        emit_c(4)
                s1, s2 = (2 * ec) % 3, (2 * ec + 1) % 3
                for (sl2, pi2) in ((s1, 1), (s2, 2)):
                    p.wait("sync", "pe", rql_free_pe[sl2])
                    p.wait("sync", "gp0", store_done[(pi2, ec)])
                    lv = p.dma("sync", "ld0", rql[sl2][:], s_wqkv[pi2, ec])
                p.wait("tensor", "ld0", lv)
                p.wait("tensor", "dv", ps_free_dv[0])
                p.wait("tensor", "ac", ps_free_ac[0])
                for (sl2, off) in ((s1, 0), (s2, 1024)):
                    for d in range(16):
                        for h2 in range(2):
                            p.mm(lambda t, sl2=sl2, off=off, d=d, h2=h2: t.matmul(
                                psA[:, off + h2 * 512: off + (h2 + 1) * 512],
                                rql[sl2][:, d * 128:(d + 1) * 128],
                                xnT[:, d * GT + h2 * 512: d * GT + (h2 + 1) * 512],
                                start=(d == 0), stop=(d == 15)),
                                inc=1 if (d == 15 and h2 == 1) else 0)
                    rql_free_pe[sl2] = p.c["pe"]
                pv = p.c["pe"]
                p.wait("scalar", "pe", pv)
                avk = p.op("scalar", "ac", lambda s: s.copy(ktmp[:], psA[:, 0:1024]))
                p.wait("vector", "pe", pv)
                p.wait("vector", "ac", avk)
                p.op("vector", "dv", lambda v: v.tensor_tensor(
                    kvb[:], psA[:, 1024:2048], ktmp[:], ALU.mult))
                p.op("vector", "dv", lambda v: v.tensor_reduce(tmp1[:], kvb[:], AX, ALU.add))
                cs, cd = (carA, carB) if carry_par == 0 else (carB, carA)
                dvv = p.op("vector", "dv", lambda v, cs=cs, cd=cd, ec=ec: v.tensor_tensor(
                    cd[:, ec:ec + 1], cs[:, ec:ec + 1], tmp1[:], ALU.add))
                ps_free_dv[0] = dvv
                ps_free_ac[0] = avk
            if pgi == 0:
                emit_fin_a(0)
            elif pgi == 1:
                emit_fin_a(1)
            elif pgi == 2:
                emit_fin_a(2)
            carry_par = 1 - carry_par
        carry = carA if carry_par == 0 else carB
        emit_c(len(c_jobs))
        c_last_dv = p.c["dv"]
        gp0_final = p.c["gp0"]
        gp0_after_A = store_done[("wo", EC - 1)]
        gp0_after_w12 = c_marks["w12"]
        ps_free_ac[1] = bc_free_ac[0]

        m5_prev_pe_end = 0
        vh_done = [0] * EC
        vh_prev_last = 0
        m2_evac_q = 0
        m2_evac_k = 0
        m2_evac_kv = 0
        for g in range(G):
            if g == 0:
                p.wait("scalar", "gp0", gp0_final)
                p.wait("scalar", "dv", c_last_dv)
            for sub in range(NSUB):
                ln_sub(xin[(g * NSUB + sub) * 128:(g * NSUB + sub + 1) * 128, :],
                       sub, True, sub * D, "gpsimd", "ldx", sub % 2)
            p.wait("vector", "pe", m5_prev_pe_end)
            if g == 0:
                p.wait("vector", "gp0", gp0_after_A)
            for ec in range(EC):
                sl_q, sl_k, sl_v = (3 * ec) % 3, (3 * ec + 1) % 3, (3 * ec + 2) % 3
                for (sl2, pi2) in ((sl_q, 0), (sl_k, 1), (sl_v, 2)):
                    p.wait("sync", "pe", rql_free_pe[sl2])
                    if g == 0:
                        p.wait("sync", "gp0", store_done[(pi2, ec)])
                    p.dma("sync", "ldm", rql[sl2][:], s_wqkv[pi2, ec])
                lv = p.c["ldm"]
                p.wait("tensor", "ldm", lv - 32)
                if ec == 0:
                    p.wait("tensor", "ac", ps_free_ac[0])
                    p.wait("tensor", "ac", ps_free_ac[1])
                    p.wait("tensor", "dv", ps_free_dv[0])
                    p.wait("tensor", "dv", ps_free_dv[1])
                else:
                    p.wait("tensor", "ac", m2_evac_q)
                for d in range(16):
                    for h2 in range(2):
                        p.mm(lambda t, sl2=sl_q, d=d, h2=h2: t.matmul(
                            psA[:, h2 * 512:(h2 + 1) * 512],
                            rql[sl2][:, d * 128:(d + 1) * 128],
                            xnT[:, d * GT + h2 * 512: d * GT + (h2 + 1) * 512],
                            start=(d == 0), stop=(d == 15)),
                            inc=1 if (d == 15 and h2 == 1) else 0)
                pq = p.c["pe"]
                rql_free_pe[sl_q] = pq
                p.wait("tensor", "ldm", lv - 16)
                p.wait("tensor", "ac", m2_evac_k)
                for d in range(16):
                    for h2 in range(2):
                        p.mm(lambda t, sl2=sl_k, d=d, h2=h2: t.matmul(
                            psA[:, 1024 + h2 * 512: 1024 + (h2 + 1) * 512],
                            rql[sl2][:, d * 128:(d + 1) * 128],
                            xnT[:, d * GT + h2 * 512: d * GT + (h2 + 1) * 512],
                            start=(d == 0), stop=(d == 15)),
                            inc=1 if (d == 15 and h2 == 1) else 0)
                pk = p.c["pe"]
                rql_free_pe[sl_k] = pk
                p.wait("tensor", "ldm", lv)
                p.wait("tensor", "dv", m2_evac_kv)
                for d in range(16):
                    for h2 in range(2):
                        p.mm(lambda t, sl2=sl_v, d=d, h2=h2: t.matmul(
                            psB[:, h2 * 512:(h2 + 1) * 512],
                            rql[sl2][:, d * 128:(d + 1) * 128],
                            xnT[:, d * GT + h2 * 512: d * GT + (h2 + 1) * 512],
                            start=(d == 0), stop=(d == 15)),
                            inc=1 if (d == 15 and h2 == 1) else 0)
                pv_ = p.c["pe"]
                rql_free_pe[sl_v] = pv_
                p.wait("scalar", "pe", pq)
                p.wait("scalar", "dv", vh_done[ec - 1] if ec else vh_prev_last)
                m2_evac_q = p.op("scalar", "ac", lambda s: s.activation(
                    qtmp[:], psA[:, 0:1024], AF.Copy, scale=DENOM))
                p.wait("scalar", "pe", pk)
                p.wait("scalar", "dv", m2_evac_kv)
                m2_evac_k = p.op("scalar", "ac", lambda s: s.copy(ktmp[:], psA[:, 1024:2048]))
                p.wait("vector", "pe", pv_)
                p.wait("vector", "ac", m2_evac_k)
                m2_evac_kv = p.op("vector", "dv", lambda v: v.tensor_tensor(
                    kvb[:], psB[:, 0:1024], ktmp[:], ALU.mult))
                p.op("vector", "dv", lambda v, ec=ec, carry=carry: v.tensor_tensor_scan(
                    kvt[:], sones[:], kvb[:], carry[:, ec:ec + 1], ALU.mult, ALU.add))
                p.op("vector", "dv", lambda v, ec=ec, carry=carry: v.tensor_copy(
                    carry[:, ec:ec + 1], kvt[:, GT - 1:GT]))
                p.wait("vector", "ac", m2_evac_q)
                vh_done[ec] = p.op("vector", "dv", lambda v, ec=ec: v.tensor_tensor(
                    vhat[:, ec * GT:(ec + 1) * GT], kvt[:], qtmp[:], ALU.mult))
            vh_prev_last = vh_done[EC - 1]
            m3_evac = [0] * NSUB
            for fq in range(FQ):
                if fq == 0:
                    p.wait("tensor", "ac", m2_evac_k)
                    p.wait("tensor", "dv", m2_evac_kv)
                    if g == 0:
                        p.wait("sync", "gp0", gp0_after_A)
                for ec in range(EC):
                    sl2 = (fq * EC + ec) % 3
                    p.wait("sync", "pe", rwo_free_pe[sl2])
                    lv = p.dma("sync", "ldm", rwo[sl2][:], s_wo[fq, ec])
                    p.wait("tensor", "ldm", lv)
                    p.wait("tensor", "dv", vh_done[ec])
                    for sub in range(NSUB):
                        if fq and ec == 0:
                            p.wait("tensor", "dv", m3_evac[sub])
                        ps = psA if sub < 4 else psB
                        po = (sub % 4) * 512
                        p.mm(lambda t, sub=sub, ec=ec, sl2=sl2, ps=ps, po=po: t.matmul(
                            ps[:, po:po + 512],
                            vhat[:, ec * GT + sub * 128: ec * GT + (sub + 1) * 128],
                            rwo[sl2][:], start=(ec == 0), stop=(ec == EC - 1)),
                            inc=1 if (sub == NSUB - 1 or ec == EC - 1) else 0)
                    rwo_free_pe[sl2] = p.c["pe"]
                pe_end3 = p.c["pe"]
                for sub in range(NSUB):
                    ps = psA if sub < 4 else psB
                    po = (sub % 4) * 512
                    r0 = sub * D + fq * 512
                    p.wait("vector", "pe", pe_end3 - NSUB + sub + 1)
                    p.wait("vector", "ac", acc_copy_ac[sub])
                    m3_evac[sub] = p.op("vector", "dv", lambda v, ps=ps, po=po, r0=r0:
                                        v.tensor_tensor(acc[:, r0:r0 + 512],
                                                        acc[:, r0:r0 + 512],
                                                        ps[:, po:po + 512], ALU.add))
            if g == 0:
                p.wait("sync", "gp0", gp0_after_w12)
            hid_done = [0] * HC
            sig_prev = [0, 0]
            for hc in range(HC):
                par = hc % 2
                ps = psX[par]
                sl_a, sl_b = (2 * hc) % 3, (2 * hc + 1) % 3
                for (sl2, sm) in ((sl_a, s_w1), (sl_b, s_w2)):
                    p.wait("sync", "pe", rql_free_pe[sl2])
                    p.dma("sync", "ldm", rql[sl2][:], sm[hc])
                lv = p.c["ldm"]
                p.wait("tensor", "ldm", lv)
                if hc < 2:
                    for sub in range(NSUB):
                        p.wait("tensor", "dv", m3_evac[sub])
                else:
                    p.wait("tensor", "ac", sig_prev[par])
                    p.wait("tensor", "dv", hid_done[hc - 2])
                for d in range(16):
                    for h2 in range(2):
                        p.mm(lambda t, sl2=sl_a, d=d, h2=h2, ps=ps: t.matmul(
                            ps[:, h2 * 512:(h2 + 1) * 512],
                            rql[sl2][:, d * 128:(d + 1) * 128],
                            xnT[:, d * GT + h2 * 512: d * GT + (h2 + 1) * 512],
                            start=(d == 0), stop=(d == 15)),
                            inc=1 if (d == 15 and h2 == 1) else 0)
                p1 = p.c["pe"]
                for d in range(16):
                    for h2 in range(2):
                        p.mm(lambda t, sl2=sl_b, d=d, h2=h2, ps=ps: t.matmul(
                            ps[:, 1024 + h2 * 512: 1024 + (h2 + 1) * 512],
                            rql[sl2][:, d * 128:(d + 1) * 128],
                            xnT[:, d * GT + h2 * 512: d * GT + (h2 + 1) * 512],
                            start=(d == 0), stop=(d == 15)),
                            inc=1 if (d == 15 and h2 == 1) else 0)
                p2 = p.c["pe"]
                rql_free_pe[sl_a] = p1
                rql_free_pe[sl_b] = p2
                p.wait("scalar", "pe", p1)
                if hc:
                    p.wait("scalar", "dv", hid_done[hc - 1])
                sv = p.op("scalar", "ac", lambda s, ps=ps: s.activation(
                    sgt[:], ps[:, 0:1024], AF.Sigmoid))
                sig_prev[par] = sv
                p.wait("vector", "pe", p2)
                p.wait("vector", "ac", sv)
                if hc == 0:
                    p.wait("vector", "pe", pe_end3)
                hid_done[hc] = p.op("vector", "dv", lambda v, hc=hc, ps=ps: v.tensor_tensor(
                    hidv[:, hc * GT:(hc + 1) * GT], ps[:, 1024:2048], sgt[:], ALU.mult))
            m5_evac = [0] * NSUB
            for fq in range(FQ):
                if fq == 0:
                    p.wait("tensor", "ac", sig_prev[0])
                    p.wait("tensor", "ac", sig_prev[1])
                    p.wait("tensor", "dv", hid_done[HC - 1])
                    p.wait("tensor", "dv", hid_done[HC - 2])
                    if g == 0:
                        p.wait("sync", "gp0", gp0_final)
                for hc in range(HC):
                    sl2 = (fq * HC + hc) % 3
                    p.wait("sync", "pe", rwo_free_pe[sl2])
                    lv = p.dma("sync", "ldm", rwo[sl2][:], s_w3[fq, hc])
                    p.wait("tensor", "ldm", lv)
                    if fq == 0:
                        p.wait("tensor", "dv", hid_done[hc])
                    for sub in range(NSUB):
                        if fq and hc == 0:
                            p.wait("tensor", "dv", m5_evac[sub])
                        ps = psA if sub < 4 else psB
                        po = (sub % 4) * 512
                        p.mm(lambda t, sub=sub, hc=hc, sl2=sl2, ps=ps, po=po: t.matmul(
                            ps[:, po:po + 512],
                            hidv[:, hc * GT + sub * 128: hc * GT + (sub + 1) * 128],
                            rwo[sl2][:], start=(hc == 0), stop=(hc == HC - 1)),
                            inc=1 if (sub == NSUB - 1 or hc == HC - 1) else 0)
                    rwo_free_pe[sl2] = p.c["pe"]
                pe_end5 = p.c["pe"]
                for sub in range(NSUB):
                    ps = psA if sub < 4 else psB
                    po = (sub % 4) * 512
                    r0 = sub * D + fq * 512
                    osl = (fq * NSUB + sub) % 2
                    p.wait("vector", "pe", pe_end5 - NSUB + sub + 1)
                    p.wait("vector", "gpm", outst_free_gp[osl])
                    ev = p.op("vector", "dv", lambda v, ps=ps, po=po, r0=r0, osl=osl:
                              v.tensor_tensor(outst[osl][:], ps[:, po:po + 512],
                                              acc[:, r0:r0 + 512], ALU.add))
                    m5_evac[sub] = ev
                    if fq == FQ - 1:
                        acc_free_dv[sub] = ev
                    rr = (g * NSUB + sub) * 128
                    p.wait("scalar", "dv", ev)
                    gv = p.dma("scalar", "gpm",
                               out[rr:rr + 128, fq * 512:(fq + 1) * 512], outst[osl][:])
                    outst_free_gp[osl] = gv
            m2_evac_q = 0
            m2_evac_k = 0
            m2_evac_kv = 0
            ps_free_ac = [0, 0]
            ps_free_dv = [m5_evac[3], m5_evac[7]]
            m5_prev_pe_end = p.c["pe"]

        p.wait("gpsimd", "gpm", p.c["gpm"])
        p.wait("gpsimd", "gp0", p.c["gp0"])

        with nc.Block() as block:
            @block.sync
            def _(q):
                for fn in p.ops["sync"]:
                    fn(q)

            @block.tensor
            def _(q):
                for fn in p.ops["tensor"]:
                    fn(q)

            @block.vector
            def _(q):
                for fn in p.ops["vector"]:
                    fn(q)

            @block.scalar
            def _(q):
                for fn in p.ops["scalar"]:
                    fn(q)

            @block.gpsimd
            def _(q):
                for fn in p.ops["gpsimd"]:
                    fn(q)

    return nc


_NC_CACHE = [None]


def kernel(L0_attn_x, L0_attn_Wq, L0_attn_Wk, L0_attn_Wv, L0_attn_Wo,
           L0_ffn_w1, L0_ffn_w2, L0_ffn_w3):
    x = np.ascontiguousarray(np.asarray(L0_attn_x, dtype=np.float32))
    Wq = np.asarray(L0_attn_Wq, dtype=np.float32)
    Wk = np.asarray(L0_attn_Wk, dtype=np.float32)
    Wv = np.asarray(L0_attn_Wv, dtype=np.float32)
    Wo = np.asarray(L0_attn_Wo, dtype=np.float32)
    W1 = np.asarray(L0_ffn_w1, dtype=np.float32)
    W2 = np.asarray(L0_ffn_w2, dtype=np.float32)
    W3 = np.asarray(L0_ffn_w3, dtype=np.float32)

    def blk_sq(W):
        return np.ascontiguousarray(
            W.reshape(16, 128, 16, 128).transpose(0, 3, 2, 1).reshape(16, 128, 2048))

    def blk_512(W, R):
        return np.ascontiguousarray(
            W.reshape(4, 512, R, 128).transpose(2, 3, 0, 1).reshape(R, 128, 2048))

    def blk_h(W, R):
        return np.ascontiguousarray(
            W.reshape(R, 128, 16, 128).transpose(0, 3, 2, 1).reshape(R, 128, 2048))

    ws = {
        "wq_b": blk_sq(Wq),
        "wk_b": blk_sq(Wk),
        "wv_b": blk_sq(Wv),
        "wo_b": blk_512(Wo, 16),
        "w1_b": blk_h(W1, 44),
        "w2_b": blk_h(W2, 44),
        "w3_b": blk_512(W3, 44),
        "eye_bi": np.eye(128, dtype=ml_dtypes.bfloat16),
        "ones_i": np.ones((1, 128), dtype=np.float32),
    }
    zeros = np.zeros((T, D), dtype=np.float32)
    in_maps = []
    for c in range(8):
        b, h = c // 2, c % 2
        m = dict(ws)
        m["xin"] = np.ascontiguousarray(x[b, h * T:(h + 1) * T, :])
        m["xprev"] = np.ascontiguousarray(x[b, 0:T, :]) if h == 1 else zeros
        in_maps.append(m)

    if _NC_CACHE[0] is None:
        _NC_CACHE[0] = build_nc()
    nc = _NC_CACHE[0]
    res = run_bass_kernel_spmd(nc, in_maps, core_ids=list(range(8)))
    outp = np.empty((NB, SEQ, D), dtype=np.float32)
    for c in range(8):
        b, h = c // 2, c % 2
        outp[b, h * T:(h + 1) * T, :] = res.results[c]["out"]
    return outp


# version 11
# speedup vs baseline: 1.4345x; 1.2165x over previous
import numpy as np
import ml_dtypes
from contextlib import ExitStack

import concourse.bass as bass
import concourse.mybir as mybir
from concourse.bass_isa import ReduceOp
from concourse.bass_utils import run_bass_kernel_spmd

F32 = mybir.dt.float32
BF16 = mybir.dt.bfloat16
AF = mybir.ActivationFunctionType
ALU = mybir.AluOpType
AX = mybir.AxisListType.X

D = 2048
HH = 5632
NB = 4
SEQ = 8192
T = 4096
G = 4
GT = 1024
NSUB = 8
EC = 16
HC = 44
FQ = 4
DENOM = float(np.float32(np.sqrt(np.float32(2048.0))) / np.float32(8192.0))
N_EL = float(HH * D)

ENGS = ("sync", "tensor", "vector", "scalar", "gpsimd")


class Prog:
    def __init__(self):
        self.ops = {e: [] for e in ENGS}
        self.c = {}
        self.sems = {}

    def sem(self, name):
        self.c.setdefault(name, 0)
        return name

    def raw(self, eng, fn):
        self.ops[eng].append(fn)

    def wait(self, eng, sem, val):
        if not val or val <= 0:
            return
        def f(q, sem=sem, val=val):
            q.wait_ge(self.sems[sem], val)
        self.raw(eng, f)

    def dma(self, eng, sem, out, in_):
        self.sem(sem)
        self.c[sem] += 16
        def f(q, out=out, in_=in_, sem=sem):
            q.dma_start(out=out, in_=in_).then_inc(self.sems[sem], 16)
        self.raw(eng, f)
        return self.c[sem]

    def op(self, eng, sem, fn):
        self.sem(sem)
        prev = self.c[sem]
        if prev:
            self.wait(eng, sem, prev)
        self.c[sem] += 1
        def f(q, fn=fn, sem=sem):
            fn(q).then_inc(self.sems[sem], 1)
        self.raw(eng, f)
        return self.c[sem]

    def mm(self, fn, inc=0):
        self.sem("pe")
        if inc:
            self.c["pe"] += inc
        def f(q, fn=fn, inc=inc):
            ins = fn(q)
            if inc:
                ins.then_inc(self.sems["pe"], inc)
        self.raw("tensor", f)
        return self.c["pe"]


def build_nc():
    nc = bass.Bass("TRN2", target_bir_lowering=False)

    xin = nc.dram_tensor("xin", [T, D], F32, kind="ExternalInput").ap()
    xprev = nc.dram_tensor("xprev", [T, D], F32, kind="ExternalInput").ap()
    wq_b = nc.dram_tensor("wq_b", [EC, 128, D], F32, kind="ExternalInput").ap()
    wk_b = nc.dram_tensor("wk_b", [EC, 128, D], F32, kind="ExternalInput").ap()
    wv_b = nc.dram_tensor("wv_b", [EC, 128, D], F32, kind="ExternalInput").ap()
    wo_b = nc.dram_tensor("wo_b", [EC, 128, D], F32, kind="ExternalInput").ap()
    w1_b = nc.dram_tensor("w1_b", [HC, 128, D], F32, kind="ExternalInput").ap()
    w2_b = nc.dram_tensor("w2_b", [HC, 128, D], F32, kind="ExternalInput").ap()
    w3_b = nc.dram_tensor("w3_b", [HC, 128, D], F32, kind="ExternalInput").ap()
    eye_bi = nc.dram_tensor("eye_bi", [128, 128], BF16, kind="ExternalInput").ap()
    out = nc.dram_tensor("out", [T, D], F32, kind="ExternalOutput").ap()

    s_wqkv = nc.dram_tensor("s_wqkv", [3, EC, 128, D], BF16, kind="Internal").ap()
    s_wo = nc.dram_tensor("s_wo", [FQ, EC, 128, 512], BF16, kind="Internal").ap()
    s_w1 = nc.dram_tensor("s_w1", [HC, 128, D], BF16, kind="Internal").ap()
    s_w2 = nc.dram_tensor("s_w2", [HC, 128, D], BF16, kind="Internal").ap()
    s_w3 = nc.dram_tensor("s_w3", [FQ, HC, 128, 512], BF16, kind="Internal").ap()

    p = Prog()

    with ExitStack() as ctx:
        e = ctx.enter_context

        xnT = e(nc.sbuf_tensor("xnT", [128, EC * GT], BF16))
        hidv = e(nc.sbuf_tensor("hidv", [128, HC * GT], BF16))
        acc = e(nc.sbuf_tensor("acc", [128, NSUB * D], BF16))
        x_f = [e(nc.sbuf_tensor(f"x_f{i}", [128, D], F32)) for i in range(2)]
        xtk2 = [e(nc.sbuf_tensor(f"xtk{i}", [128, D], BF16)) for i in range(2)]
        rql = [e(nc.sbuf_tensor(f"rql{i}", [128, D], BF16)) for i in range(3)]
        rwo = [e(nc.sbuf_tensor(f"rwo{i}", [128, 512], BF16)) for i in range(3)]
        outst = [e(nc.sbuf_tensor(f"outst{i}", [128, 512], F32)) for i in range(2)]
        sgt = e(nc.sbuf_tensor("sgt", [128, GT], BF16))
        eyeB = e(nc.sbuf_tensor("eyeB", [128, 128], BF16))
        onesF = e(nc.sbuf_tensor("onesF", [128, 128], F32))
        sones = e(nc.sbuf_tensor("sones", [128, GT], BF16))
        carA = e(nc.sbuf_tensor("carA", [128, EC], F32))
        carB = e(nc.sbuf_tensor("carB", [128, EC], F32))
        tall = e(nc.sbuf_tensor("tall", [128, 3], F32))
        tneg = e(nc.sbuf_tensor("tneg", [128, 3], F32))
        epst = e(nc.sbuf_tensor("epst", [128, 1], F32))
        stats = e(nc.sbuf_tensor("stats", [128, 24], F32))
        mv = e(nc.sbuf_tensor("mv", [128, 2 * NSUB], F32))
        rstd = e(nc.sbuf_tensor("rstd", [128, NSUB], F32))
        pg = e(nc.sbuf_tensor("pg", [128, 136], F32))
        pgs = e(nc.sbuf_tensor("pgs", [128, 1], F32))
        tmp1 = e(nc.sbuf_tensor("tmp1", [128, 1], F32))

        psA = e(nc.psum_tensor("psA", [128, 2048], F32))
        psB = e(nc.psum_tensor("psB", [128, 2048], F32))
        psX = [psA, psB]
        vhat = hidv
        af = [hidv[:, 6144:10240].bitcast(F32), hidv[:, 10240:14336].bitcast(F32),
              hidv[:, 24576:28672].bitcast(F32)]
        ast = [hidv[:, 0:2048], hidv[:, 2048:4096], hidv[:, 4096:6144]]
        rql4 = hidv[:, 14336:16384]
        cf = [acc[:, 6144:10240].bitcast(F32), acc[:, 10240:14336].bitcast(F32),
              hidv[:, 16384:20480].bitcast(F32), hidv[:, 20480:24576].bitcast(F32)]
        qtmp = hidv[:, 40960:41984]
        ktmp = hidv[:, 41984:43008]
        kvb = hidv[:, 43008:44032]
        kvt = hidv[:, 44032:45056]

        sem_names = (["lde", "pe", "dv", "ac"]
                     + [f"lda{i}" for i in range(3)] + [f"sta{i}" for i in range(3)]
                     + [f"ldp{i}" for i in range(2)] + [f"ldx{i}" for i in range(2)]
                     + [f"ldq{i}" for i in range(4)]
                     + [f"ldw{i}" for i in range(3)] + [f"ldc{i}" for i in range(4)]
                     + [f"stc{i}" for i in range(2)] + [f"sto{i}" for i in range(2)])
        for s in sem_names:
            p.sems[s] = e(nc.semaphore(s))
            p.sem(s)

        p.dma("sync", "lde", eyeB[:], eye_bi)
        p.wait("vector", "lde", 16)
        p.op("vector", "dv", lambda v: v.memset(onesF[:], 1.0))
        p.op("vector", "dv", lambda v: v.memset(epst[:], 1e-5))
        p.op("vector", "dv", lambda v: v.memset(sones[:], 1.0))
        p.op("vector", "dv", lambda v: v.memset(carA[:], 0.0))
        p.op("vector", "dv", lambda v: v.memset(carB[:], 0.0))

        af_free_ac = [0, 0, 0]
        ast_free_gp = [0, 0, 0]
        cf_free_dv = [0, 0, 0, 0]
        cf_idx = [0]
        cstg_free = [0, 0]
        xf_free_dv = [0, 0]
        xf_free_pl = [0, 0]
        rql_free_pe = [0, 0, 0, 0]
        rwo_free_pe = [0, 0, 0]
        a_cast_last = [0]
        outst_free_gp = [0, 0]
        ps_free_ac = [0, 0]
        ps_free_dv = [0, 0]
        acc_free_dv = [0] * NSUB
        acc_copy_ac = [0] * NSUB
        xtk_free_pe = [0, 0]
        store_done = {}

        a_jobs = []
        for ec in range(EC):
            a_jobs.append((wk_b[ec], 1, ec))
            a_jobs.append((wv_b[ec], 2, ec))
        for ec in range(EC):
            a_jobs.append((wq_b[ec], 0, ec))
        for ec in range(EC):
            a_jobs.append((wo_b[ec], 3, ec))
        a_next = [0]

        def emit_a(n):
            while n > 0 and a_next[0] < len(a_jobs):
                src, pi, ec = a_jobs[a_next[0]]
                idx = a_next[0]
                a_next[0] += 1
                n -= 1
                sl = idx % 3
                stg = idx % 3
                p.wait("sync", "ac", af_free_ac[sl])
                lv = p.dma("sync", f"lda{sl}", af[sl][:], src)
                p.wait("scalar", f"lda{sl}", lv)
                p.wait("scalar", f"sta{stg}", ast_free_gp[stg])
                av = p.op("scalar", "ac", lambda s, sl=sl, stg=stg: s.copy(
                    ast[stg][:], af[sl][:]))
                af_free_ac[sl] = av
                a_cast_last[0] = av
                p.wait("gpsimd", "ac", av)
                if pi < 3:
                    gv = p.dma("gpsimd", f"sta{stg}", s_wqkv[pi, ec], ast[stg][:])
                    store_done[(pi, ec)] = (f"sta{stg}", gv)
                else:
                    for fq in range(FQ):
                        gv = p.dma("gpsimd", f"sta{stg}", s_wo[fq, ec],
                                   ast[stg][:, fq * 512:(fq + 1) * 512])
                    store_done[("wo", ec)] = (f"sta{stg}", gv)
                ast_free_gp[stg] = gv

        gamma_jobs = []
        for mi, W in enumerate((w1_b, w2_b, w3_b)):
            for hc in range(HC):
                gamma_jobs.append((W[hc], mi * HC + hc))
        g_next = [0]

        def emit_gamma(n):
            while n > 0 and g_next[0] < len(gamma_jobs):
                src, gcol = gamma_jobs[g_next[0]]
                g_next[0] += 1
                n -= 1
                sl = cf_idx[0] % 4
                cf_idx[0] += 1
                p.wait("scalar", "dv", cf_free_dv[sl])
                lv = p.dma("scalar", f"ldc{sl}", cf[sl][:], src)
                p.wait("vector", f"ldc{sl}", lv)
                dvv = p.op("vector", "dv", lambda v, sl=sl, gcol=gcol:
                           v.tensor_reduce(pg[:, gcol:gcol + 1], cf[sl][:],
                                           AX, ALU.add, apply_absolute_value=True))
                cf_free_dv[sl] = dvv

        fin_mm_pe = [0]
        fin_red_dv = [0]

        def emit_fin_a(mi):
            c0, c1 = mi * HC, (mi + 1) * HC
            p.wait("vector", "pe", fin_mm_pe[0])
            fin_red_dv[0] = p.op("vector", "dv", lambda v, c0=c0, c1=c1: v.tensor_reduce(
                pgs[:], pg[:, c0:c1], AX, ALU.add))

        def emit_fin_b(mi):
            p.wait("tensor", "dv", fin_red_dv[0])
            p.wait("tensor", "dv", ps_free_dv[0])
            p.wait("tensor", "ac", ps_free_ac[0])
            p.mm(lambda t: t.matmul(psA[:, 0:1], onesF[:], pgs[:],
                                    start=True, stop=True), inc=1)
            fin_mm_pe[0] = p.c["pe"]
            p.wait("vector", "pe", p.c["pe"])
            p.op("vector", "dv", lambda v, mi=mi: v.tensor_scalar(
                tall[:, mi:mi + 1], psA[:, 0:1], 0.5 / N_EL, 0.5e-8, ALU.mult, ALU.add))
            dvv = p.op("vector", "dv", lambda v, mi=mi: v.tensor_scalar(
                tneg[:, mi:mi + 1], tall[:, mi:mi + 1], -1.0, None, ALU.mult))
            ps_free_dv[0] = dvv

        c_jobs = []
        for mi, (W, dsts) in enumerate([
            (w1_b, None), (w2_b, None), (w3_b, True),
        ]):
            for hc in range(HC):
                if mi == 0:
                    dl = [(s_w1[hc], 0, D)]
                elif mi == 1:
                    dl = [(s_w2[hc], 0, D)]
                else:
                    dl = [(s_w3[fq, hc], fq * 512, 512) for fq in range(FQ)]
                c_jobs.append((W[hc], mi, dl))
        c_next = [0]
        c_marks = {}

        def emit_c(n):
            while n > 0 and c_next[0] < len(c_jobs):
                src, m, dl = c_jobs[c_next[0]]
                cidx = c_next[0]
                c_next[0] += 1
                n -= 1
                sl = cf_idx[0] % 4
                cf_idx[0] += 1
                stg = cidx % 2
                p.wait("scalar", "dv", cf_free_dv[sl])
                lv = p.dma("scalar", f"ldc{sl}", cf[sl][:], src)
                p.wait("vector", f"ldc{sl}", lv)
                p.wait("vector", f"stc{stg}", cstg_free[stg])
                soff = 2048 * (1 + stg)
                p.op("vector", "dv", lambda v, sl=sl, m=m: v.tensor_scalar(
                    acc[:, 0:2048], cf[sl][:], tneg[:, m:m + 1], None, ALU.is_lt))
                dvv = p.op("vector", "dv", lambda v, sl=sl, m=m, soff=soff:
                           v.scalar_tensor_tensor(acc[:, soff:soff + 2048], cf[sl][:],
                                                  tall[:, m:m + 1], acc[:, 0:2048],
                                                  ALU.is_gt, ALU.subtract))
                cf_free_dv[sl] = dvv
                p.wait("gpsimd", "dv", dvv)
                for (dst, coff, w) in dl:
                    gv = p.dma("gpsimd", f"stc{stg}", dst,
                               acc[:, soff + coff: soff + coff + w])
                cstg_free[stg] = gv
                if cidx == 2 * HC - 1:
                    c_marks["w12"] = [("stc0", p.c["stc0"]), ("stc1", p.c["stc1"])]

        def ln_sub(src_rows, sub, copy_acc, acc_off, eng, par):
            sl = sub % 2
            lsem = f"ldx{sl}" if eng == "gpsimd" else f"ldp{sl}"
            p.wait(eng, "dv", xf_free_dv[sl])
            p.wait(eng, "ac", xf_free_pl[sl])
            lv = p.dma(eng, lsem, x_f[sl][:], src_rows)
            p.wait("vector", lsem, lv)
            for j in range(4):
                p.op("vector", "dv", lambda v, sl=sl, j=j: v.bn_stats(
                    stats[:, j * 6:(j + 1) * 6], x_f[sl][:, j * 512:(j + 1) * 512]))
            p.op("vector", "dv", lambda v, sub=sub: v.bn_aggr(
                mv[:, 2 * sub:2 * sub + 2], stats[:]))
            p.wait("scalar", "dv", p.c["dv"])
            av = p.op("scalar", "ac", lambda s, sub=sub: s.activation(
                rstd[:, sub:sub + 1], mv[:, 2 * sub + 1:2 * sub + 2], AF.Sqrt,
                bias=epst[:, 0:1]))
            if copy_acc:
                p.wait("scalar", lsem, lv)
                p.wait("scalar", "dv", acc_free_dv[sub])
                plv = p.op("scalar", "ac", lambda s, sl=sl, acc_off=acc_off:
                           s.copy(acc[:, acc_off:acc_off + D], x_f[sl][:]))
                xf_free_pl[sl] = plv
                acc_copy_ac[sub] = plv
            p.wait("vector", "ac", av)
            p.op("vector", "dv", lambda v, sub=sub: v.reciprocal(
                rstd[:, sub:sub + 1], rstd[:, sub:sub + 1]))
            xt = xtk2[sub % 2]
            p.wait("vector", "pe", xtk_free_pe[sub % 2])
            dvv = p.op("vector", "dv", lambda v, sl=sl, sub=sub, xt=xt: v.tensor_scalar(
                xt[:], x_f[sl][:], mv[:, 2 * sub:2 * sub + 1], rstd[:, sub:sub + 1],
                ALU.subtract, ALU.mult))
            xf_free_dv[sl] = dvv
            p.wait("tensor", "dv", dvv)
            p.wait("tensor", "ac", ps_free_ac[par])
            p.wait("tensor", "dv", ps_free_dv[par])
            for c in range(16):
                p.mm(lambda t, c=c, par=par, xt=xt: t.matmul(
                    psX[par][:, c * 128:(c + 1) * 128], xt[:, c * 128:(c + 1) * 128],
                    eyeB[:], start=True, stop=True), inc=1 if c == 15 else 0)
            pv = p.c["pe"]
            xtk_free_pe[sub % 2] = pv
            p.wait("scalar", "pe", pv)
            av2 = p.op("scalar", "ac", lambda s, par=par, sub=sub: s.copy(
                xnT[:].rearrange("p (c s) -> p c s", c=EC)[:, :, sub * 128:(sub + 1) * 128],
                psX[par][:].rearrange("p (c s) -> p c s", c=EC)))
            ps_free_ac[par] = av2

        emit_a(4)
        for sub in range(NSUB):
            ln_sub(xprev[sub * 128:(sub + 1) * 128, :], sub, False, 0, "sync", sub % 2)
            emit_a(1)
        carry_par = 0
        last_kvb_dv = 0
        for pgi in range(G):
            if pgi > 0:
                for sub in range(NSUB):
                    ln_sub(xprev[(pgi * NSUB + sub) * 128:(pgi * NSUB + sub + 1) * 128, :],
                           sub, False, 0, "sync", sub % 2)
                emit_fin_b(pgi - 1)
            rql_l = rql + [rql4]
            for ec in range(EC):
                if pgi == 0:
                    want = 2 * (ec + 1) + max(0, 2 * (ec - 7))
                    emit_a(max(0, want - a_next[0]))
                par = ec % 2
                ps = psX[par]
                s1, s2 = (2 * ec) % 4, (2 * ec + 1) % 4
                for (sl2, pi2) in ((s1, 1), (s2, 2)):
                    p.wait("sync", "pe", rql_free_pe[sl2])
                    sd_sem, sd_cnt = store_done[(pi2, ec)]
                    p.wait("sync", sd_sem, sd_cnt)
                    lv = p.dma("sync", f"ldq{sl2}", rql_l[sl2][:], s_wqkv[pi2, ec])
                    p.wait("tensor", f"ldq{sl2}", lv)
                p.wait("tensor", "dv", ps_free_dv[par])
                p.wait("tensor", "ac", ps_free_ac[par])
                if ec == 0:
                    p.wait("tensor", "ac", ps_free_ac[1])
                for (sl2, off) in ((s1, 0), (s2, 1024)):
                    for d in range(16):
                        for h2 in range(2):
                            p.mm(lambda t, sl2=sl2, off=off, d=d, h2=h2, ps=ps: t.matmul(
                                ps[:, off + h2 * 512: off + (h2 + 1) * 512],
                                rql_l[sl2][:, d * 128:(d + 1) * 128],
                                xnT[:, d * GT + h2 * 512: d * GT + (h2 + 1) * 512],
                                start=(d == 0), stop=(d == 15)),
                                inc=1 if (d == 15 and h2 == 1) else 0)
                    rql_free_pe[sl2] = p.c["pe"]
                pv = p.c["pe"]
                p.wait("scalar", "pe", pv)
                p.wait("scalar", "dv", last_kvb_dv)
                avk = p.op("scalar", "ac", lambda s, ps=ps: s.copy(ktmp[:], ps[:, 0:1024]))
                p.wait("vector", "pe", pv)
                p.wait("vector", "ac", avk)
                kvdv = p.op("vector", "dv", lambda v, ps=ps: v.tensor_tensor(
                    kvb[:], ps[:, 1024:2048], ktmp[:], ALU.mult))
                p.op("vector", "dv", lambda v: v.tensor_reduce(tmp1[:], kvb[:], AX, ALU.add))
                cs, cd = (carA, carB) if carry_par == 0 else (carB, carA)
                p.op("vector", "dv", lambda v, cs=cs, cd=cd, ec=ec: v.tensor_tensor(
                    cd[:, ec:ec + 1], cs[:, ec:ec + 1], tmp1[:], ALU.add))
                ps_free_dv[par] = kvdv
                ps_free_ac[par] = avk
                last_kvb_dv = kvdv
                if pgi > 0:
                    emit_a(4)
                emit_gamma(min(3 * HC, pgi * HC + (ec + 1) * HC // EC) - g_next[0])
                if pgi > 0:
                    emit_c(min(pgi * HC, (pgi - 1) * HC + (ec + 1) * HC // EC)
                           - c_next[0])
            if pgi < 3:
                emit_fin_a(pgi)
            carry_par = 1 - carry_par
        carry = carA if carry_par == 0 else carB
        emit_c(len(c_jobs))
        c_last_dv = p.c["dv"]
        st_final = [("stc0", p.c["stc0"]), ("stc1", p.c["stc1"])]
        st_after_A = [(f"sta{i}", p.c[f"sta{i}"]) for i in range(3)]
        st_w12 = c_marks["w12"]

        m5_prev_pe_end = p.c["pe"]
        vh_done = [0] * EC
        vh_prev_last = 0
        m2_evac_q = 0
        m2_evac_k = 0
        m2_evac_kv = 0
        for g in range(G):
            if g == 0:
                for (sm, cv) in st_final:
                    p.wait("scalar", sm, cv)
                p.wait("scalar", "dv", c_last_dv)
            for sub in range(NSUB):
                ln_sub(xin[(g * NSUB + sub) * 128:(g * NSUB + sub + 1) * 128, :],
                       sub, True, sub * D, "gpsimd", sub % 2)
            p.wait("vector", "pe", m5_prev_pe_end)
            if g == 0:
                for (sm, cv) in st_after_A:
                    p.wait("vector", sm, cv)
            for ec in range(EC):
                sl_q, sl_k, sl_v = (3 * ec) % 3, (3 * ec + 1) % 3, (3 * ec + 2) % 3
                lvs = {}
                for (sl2, pi2) in ((sl_q, 0), (sl_k, 1), (sl_v, 2)):
                    p.wait("sync", "pe", rql_free_pe[sl2])
                    if g == 0:
                        sd_sem, sd_cnt = store_done[(pi2, ec)]
                        p.wait("sync", sd_sem, sd_cnt)
                    lvs[sl2] = p.dma("sync", f"ldq{sl2}", rql[sl2][:], s_wqkv[pi2, ec])
                p.wait("tensor", f"ldq{sl_q}", lvs[sl_q])
                if ec == 0:
                    p.wait("tensor", "ac", ps_free_ac[0])
                    p.wait("tensor", "ac", ps_free_ac[1])
                    p.wait("tensor", "dv", ps_free_dv[0])
                    p.wait("tensor", "dv", ps_free_dv[1])
                else:
                    p.wait("tensor", "ac", m2_evac_q)
                for d in range(16):
                    for h2 in range(2):
                        p.mm(lambda t, sl2=sl_q, d=d, h2=h2: t.matmul(
                            psA[:, h2 * 512:(h2 + 1) * 512],
                            rql[sl2][:, d * 128:(d + 1) * 128],
                            xnT[:, d * GT + h2 * 512: d * GT + (h2 + 1) * 512],
                            start=(d == 0), stop=(d == 15)),
                            inc=1 if (d == 15 and h2 == 1) else 0)
                pq = p.c["pe"]
                rql_free_pe[sl_q] = pq
                p.wait("tensor", f"ldq{sl_k}", lvs[sl_k])
                p.wait("tensor", "ac", m2_evac_k)
                for d in range(16):
                    for h2 in range(2):
                        p.mm(lambda t, sl2=sl_k, d=d, h2=h2: t.matmul(
                            psA[:, 1024 + h2 * 512: 1024 + (h2 + 1) * 512],
                            rql[sl2][:, d * 128:(d + 1) * 128],
                            xnT[:, d * GT + h2 * 512: d * GT + (h2 + 1) * 512],
                            start=(d == 0), stop=(d == 15)),
                            inc=1 if (d == 15 and h2 == 1) else 0)
                pk = p.c["pe"]
                rql_free_pe[sl_k] = pk
                p.wait("tensor", f"ldq{sl_v}", lvs[sl_v])
                p.wait("tensor", "dv", m2_evac_kv)
                for d in range(16):
                    for h2 in range(2):
                        p.mm(lambda t, sl2=sl_v, d=d, h2=h2: t.matmul(
                            psB[:, h2 * 512:(h2 + 1) * 512],
                            rql[sl2][:, d * 128:(d + 1) * 128],
                            xnT[:, d * GT + h2 * 512: d * GT + (h2 + 1) * 512],
                            start=(d == 0), stop=(d == 15)),
                            inc=1 if (d == 15 and h2 == 1) else 0)
                pv_ = p.c["pe"]
                rql_free_pe[sl_v] = pv_
                p.wait("scalar", "pe", pq)
                p.wait("scalar", "dv", vh_done[ec - 1] if ec else vh_prev_last)
                m2_evac_q = p.op("scalar", "ac", lambda s: s.activation(
                    qtmp[:], psA[:, 0:1024], AF.Copy, scale=DENOM))
                p.wait("scalar", "pe", pk)
                p.wait("scalar", "dv", m2_evac_kv)
                m2_evac_k = p.op("scalar", "ac", lambda s: s.copy(ktmp[:], psA[:, 1024:2048]))
                p.wait("vector", "pe", pv_)
                p.wait("vector", "ac", m2_evac_k)
                m2_evac_kv = p.op("vector", "dv", lambda v: v.tensor_tensor(
                    kvb[:], psB[:, 0:1024], ktmp[:], ALU.mult))
                p.op("vector", "dv", lambda v, ec=ec, carry=carry: v.tensor_tensor_scan(
                    kvt[:], sones[:], kvb[:], carry[:, ec:ec + 1], ALU.mult, ALU.add))
                p.op("vector", "dv", lambda v, ec=ec, carry=carry: v.tensor_copy(
                    carry[:, ec:ec + 1], kvt[:, GT - 1:GT]))
                p.wait("vector", "ac", m2_evac_q)
                vh_done[ec] = p.op("vector", "dv", lambda v, ec=ec: v.tensor_tensor(
                    vhat[:, ec * GT:(ec + 1) * GT], kvt[:], qtmp[:], ALU.mult))
            vh_prev_last = vh_done[EC - 1]
            m3_evac = [0] * NSUB
            for fq in range(FQ):
                if fq == 0:
                    p.wait("tensor", "ac", m2_evac_k)
                    p.wait("tensor", "dv", m2_evac_kv)
                    if g == 0:
                        for (sm, cv) in st_after_A:
                            p.wait("sync", sm, cv)
                for ec in range(EC):
                    sl2 = (fq * EC + ec) % 3
                    p.wait("sync", "pe", rwo_free_pe[sl2])
                    lv = p.dma("sync", f"ldw{sl2}", rwo[sl2][:], s_wo[fq, ec])
                    p.wait("tensor", f"ldw{sl2}", lv)
                    p.wait("tensor", "dv", vh_done[ec])
                    for sub in range(NSUB):
                        if fq and ec == 0:
                            p.wait("tensor", "dv", m3_evac[sub])
                        ps = psA if sub < 4 else psB
                        po = (sub % 4) * 512
                        p.mm(lambda t, sub=sub, ec=ec, sl2=sl2, ps=ps, po=po: t.matmul(
                            ps[:, po:po + 512],
                            vhat[:, ec * GT + sub * 128: ec * GT + (sub + 1) * 128],
                            rwo[sl2][:], start=(ec == 0), stop=(ec == EC - 1)),
                            inc=1 if (sub == NSUB - 1 or ec == EC - 1) else 0)
                    rwo_free_pe[sl2] = p.c["pe"]
                pe_end3 = p.c["pe"]
                for sub in range(NSUB):
                    ps = psA if sub < 4 else psB
                    po = (sub % 4) * 512
                    r0 = sub * D + fq * 512
                    p.wait("vector", "pe", pe_end3 - NSUB + sub + 1)
                    p.wait("vector", "ac", acc_copy_ac[sub])
                    m3_evac[sub] = p.op("vector", "dv", lambda v, ps=ps, po=po, r0=r0:
                                        v.tensor_tensor(acc[:, r0:r0 + 512],
                                                        acc[:, r0:r0 + 512],
                                                        ps[:, po:po + 512], ALU.add))
            if g == 0:
                for (sm, cv) in st_w12:
                    p.wait("sync", sm, cv)
            hid_done = [0] * HC
            sig_prev = [0, 0]
            for hc in range(HC):
                par = hc % 2
                ps = psX[par]
                sl_a, sl_b = (2 * hc) % 3, (2 * hc + 1) % 3
                for (sl2, sm) in ((sl_a, s_w1), (sl_b, s_w2)):
                    p.wait("sync", "pe", rql_free_pe[sl2])
                    lv2 = p.dma("sync", f"ldq{sl2}", rql[sl2][:], sm[hc])
                    p.wait("tensor", f"ldq{sl2}", lv2)
                if hc < 2:
                    for sub in range(NSUB):
                        p.wait("tensor", "dv", m3_evac[sub])
                else:
                    p.wait("tensor", "ac", sig_prev[par])
                    p.wait("tensor", "dv", hid_done[hc - 2])
                for d in range(16):
                    for h2 in range(2):
                        p.mm(lambda t, sl2=sl_a, d=d, h2=h2, ps=ps: t.matmul(
                            ps[:, h2 * 512:(h2 + 1) * 512],
                            rql[sl2][:, d * 128:(d + 1) * 128],
                            xnT[:, d * GT + h2 * 512: d * GT + (h2 + 1) * 512],
                            start=(d == 0), stop=(d == 15)),
                            inc=1 if (d == 15 and h2 == 1) else 0)
                p1 = p.c["pe"]
                for d in range(16):
                    for h2 in range(2):
                        p.mm(lambda t, sl2=sl_b, d=d, h2=h2, ps=ps: t.matmul(
                            ps[:, 1024 + h2 * 512: 1024 + (h2 + 1) * 512],
                            rql[sl2][:, d * 128:(d + 1) * 128],
                            xnT[:, d * GT + h2 * 512: d * GT + (h2 + 1) * 512],
                            start=(d == 0), stop=(d == 15)),
                            inc=1 if (d == 15 and h2 == 1) else 0)
                p2 = p.c["pe"]
                rql_free_pe[sl_a] = p1
                rql_free_pe[sl_b] = p2
                p.wait("scalar", "pe", p1)
                if hc:
                    p.wait("scalar", "dv", hid_done[hc - 1])
                sv = p.op("scalar", "ac", lambda s, ps=ps: s.activation(
                    sgt[:], ps[:, 0:1024], AF.Sigmoid))
                sig_prev[par] = sv
                p.wait("vector", "pe", p2)
                p.wait("vector", "ac", sv)
                if hc == 0:
                    p.wait("vector", "pe", pe_end3)
                    if g == 0:
                        p.wait("vector", "ac", a_cast_last[0])
                hid_done[hc] = p.op("vector", "dv", lambda v, hc=hc, ps=ps: v.tensor_tensor(
                    hidv[:, hc * GT:(hc + 1) * GT], ps[:, 1024:2048], sgt[:], ALU.mult))
            m5_evac = [0] * NSUB
            for fq in range(FQ):
                if fq == 0:
                    p.wait("tensor", "ac", sig_prev[0])
                    p.wait("tensor", "ac", sig_prev[1])
                    p.wait("tensor", "dv", hid_done[HC - 1])
                    p.wait("tensor", "dv", hid_done[HC - 2])
                    if g == 0:
                        for (sm, cv) in st_final:
                            p.wait("sync", sm, cv)
                for hc in range(HC):
                    sl2 = (fq * HC + hc) % 3
                    p.wait("sync", "pe", rwo_free_pe[sl2])
                    lv = p.dma("sync", f"ldw{sl2}", rwo[sl2][:], s_w3[fq, hc])
                    p.wait("tensor", f"ldw{sl2}", lv)
                    if fq == 0:
                        p.wait("tensor", "dv", hid_done[hc])
                    for sub in range(NSUB):
                        if fq and hc == 0:
                            p.wait("tensor", "dv", m5_evac[sub])
                        ps = psA if sub < 4 else psB
                        po = (sub % 4) * 512
                        p.mm(lambda t, sub=sub, hc=hc, sl2=sl2, ps=ps, po=po: t.matmul(
                            ps[:, po:po + 512],
                            hidv[:, hc * GT + sub * 128: hc * GT + (sub + 1) * 128],
                            rwo[sl2][:], start=(hc == 0), stop=(hc == HC - 1)),
                            inc=1 if (sub == NSUB - 1 or hc == HC - 1) else 0)
                    rwo_free_pe[sl2] = p.c["pe"]
                pe_end5 = p.c["pe"]
                for sub in range(NSUB):
                    ps = psA if sub < 4 else psB
                    po = (sub % 4) * 512
                    r0 = sub * D + fq * 512
                    osl = (fq * NSUB + sub) % 2
                    p.wait("vector", "pe", pe_end5 - NSUB + sub + 1)
                    p.wait("vector", f"sto{osl}", outst_free_gp[osl])
                    ev = p.op("vector", "dv", lambda v, ps=ps, po=po, r0=r0, osl=osl:
                              v.tensor_tensor(outst[osl][:], ps[:, po:po + 512],
                                              acc[:, r0:r0 + 512], ALU.add))
                    m5_evac[sub] = ev
                    if fq == FQ - 1:
                        acc_free_dv[sub] = ev
                    rr = (g * NSUB + sub) * 128
                    p.wait("scalar", "dv", ev)
                    gv = p.dma("scalar", f"sto{osl}",
                               out[rr:rr + 128, fq * 512:(fq + 1) * 512], outst[osl][:])
                    outst_free_gp[osl] = gv
            m2_evac_q = 0
            m2_evac_k = 0
            m2_evac_kv = 0
            ps_free_ac = [0, 0]
            ps_free_dv = [m5_evac[3], m5_evac[7]]
            m5_prev_pe_end = p.c["pe"]

        for s in ("sto0", "sto1", "stc0", "stc1", "sta0", "sta1", "sta2"):
            p.wait("gpsimd", s, p.c[s])

        with nc.Block() as block:
            @block.sync
            def _(q):
                for fn in p.ops["sync"]:
                    fn(q)

            @block.tensor
            def _(q):
                for fn in p.ops["tensor"]:
                    fn(q)

            @block.vector
            def _(q):
                for fn in p.ops["vector"]:
                    fn(q)

            @block.scalar
            def _(q):
                for fn in p.ops["scalar"]:
                    fn(q)

            @block.gpsimd
            def _(q):
                for fn in p.ops["gpsimd"]:
                    fn(q)

    return nc


_NC_CACHE = [None]


def kernel(L0_attn_x, L0_attn_Wq, L0_attn_Wk, L0_attn_Wv, L0_attn_Wo,
           L0_ffn_w1, L0_ffn_w2, L0_ffn_w3):
    x = np.ascontiguousarray(np.asarray(L0_attn_x, dtype=np.float32))
    Wq = np.asarray(L0_attn_Wq, dtype=np.float32)
    Wk = np.asarray(L0_attn_Wk, dtype=np.float32)
    Wv = np.asarray(L0_attn_Wv, dtype=np.float32)
    Wo = np.asarray(L0_attn_Wo, dtype=np.float32)
    W1 = np.asarray(L0_ffn_w1, dtype=np.float32)
    W2 = np.asarray(L0_ffn_w2, dtype=np.float32)
    W3 = np.asarray(L0_ffn_w3, dtype=np.float32)

    def blk_sq(W):
        return np.ascontiguousarray(
            W.reshape(16, 128, 16, 128).transpose(0, 3, 2, 1).reshape(16, 128, 2048))

    def blk_512(W, R):
        return np.ascontiguousarray(
            W.reshape(4, 512, R, 128).transpose(2, 3, 0, 1).reshape(R, 128, 2048))

    def blk_h(W, R):
        return np.ascontiguousarray(
            W.reshape(R, 128, 16, 128).transpose(0, 3, 2, 1).reshape(R, 128, 2048))

    ws = {
        "wq_b": blk_sq(Wq),
        "wk_b": blk_sq(Wk),
        "wv_b": blk_sq(Wv),
        "wo_b": blk_512(Wo, 16),
        "w1_b": blk_h(W1, 44),
        "w2_b": blk_h(W2, 44),
        "w3_b": blk_512(W3, 44),
        "eye_bi": np.eye(128, dtype=ml_dtypes.bfloat16),
    }

    zeros = np.zeros((T, D), dtype=np.float32)
    in_maps = []
    for c in range(8):
        b, h = c // 2, c % 2
        m = dict(ws)
        m["xin"] = np.ascontiguousarray(x[b, h * T:(h + 1) * T, :])
        m["xprev"] = np.ascontiguousarray(x[b, 0:T, :]) if h == 1 else zeros
        in_maps.append(m)

    if _NC_CACHE[0] is None:
        _NC_CACHE[0] = build_nc()
    nc = _NC_CACHE[0]
    res = run_bass_kernel_spmd(nc, in_maps, core_ids=list(range(8)))
    outp = np.empty((NB, SEQ, D), dtype=np.float32)
    for c in range(8):
        b, h = c // 2, c % 2
        outp[b, h * T:(h + 1) * T, :] = res.results[c]["out"]
    return outp
